# version 24
# baseline (speedup 1.0000x reference)
import contextlib
import numpy as np

import concourse.bass as bass
import concourse.mybir as mybir
from concourse.bass_utils import run_bass_kernel_spmd

F32 = mybir.dt.float32
BF16 = mybir.dt.bfloat16
AF = mybir.ActivationFunctionType
ALU = mybir.AluOpType
AX = mybir.AxisListType

D = 1024
DT = 8
NCORES = 8
EPS = 1e-5
LOG_INV32 = float(np.log(1.0 / 32.0))


class Cfg:
    def __init__(self, t_b=8192, t_own=2048, h=2048, e=8):
        assert t_b % 512 == 0 and t_own % 512 == 0 and h % 256 == 0
        self.t_b, self.t_own, self.h, self.e = t_b, t_own, h, e
        self.NCH = t_b // 512
        self.KT = t_b // 128
        self.NQT = t_own // 128
        self.NQB = t_own // 512
        self.OWN_CH = t_own // 512
        self.NSPE = 4 if h >= 1024 else 2
        self.HC = (h // 128) // self.NSPE
        self.NTQ = t_own // 512
        self.UH1 = self.HC * self.NTQ
        self.UO2 = self.NQT * 2
        self.NSUB = e * self.NSPE
        self.Q2P = t_own // 512
        self.Q2L = 512
        self.Q2G = 1


def build(cfg: Cfg):
    nc = bass.Bass("TRN2", target_bir_lowering=False)
    C = cfg
    T, TB, H, E = C.t_own, C.t_b, C.h, C.e
    NCH, KT, NQT, NQB, OWN_CH = C.NCH, C.KT, C.NQT, C.NQB, C.OWN_CH
    HC, NTQ, UH1, UO2, NSUB, NSPE = C.HC, C.NTQ, C.UH1, C.UO2, C.NSUB, C.NSPE

    x_own = nc.dram_tensor("x_own", [T, D], F32, kind="ExternalInput").ap()
    x_rest = nc.dram_tensor("x_rest", [TB - T, D], F32, kind="ExternalInput").ap()
    Wq = nc.dram_tensor("Wq", [D, D], F32, kind="ExternalInput").ap()
    Wk = nc.dram_tensor("Wk", [D, D], F32, kind="ExternalInput").ap()
    Wv = nc.dram_tensor("Wv", [D, D], F32, kind="ExternalInput").ap()
    Wo = nc.dram_tensor("Wo", [D, D], F32, kind="ExternalInput").ap()
    bq = nc.dram_tensor("bq", [128, DT], F32, kind="ExternalInput").ap()
    bk = nc.dram_tensor("bk", [128, DT], F32, kind="ExternalInput").ap()
    bv = nc.dram_tensor("bv", [128, DT], F32, kind="ExternalInput").ap()
    bo = nc.dram_tensor("bo", [1, D], F32, kind="ExternalInput").ap()
    Wg = nc.dram_tensor("Wg", [128, DT * E], F32, kind="ExternalInput").ap()
    W1 = nc.dram_tensor("W1", [E, D, H], F32, kind="ExternalInput").ap()
    W2 = nc.dram_tensor("W2", [E, D, H], F32, kind="ExternalInput").ap()
    W3 = nc.dram_tensor("W3", [E, H, D], F32, kind="ExternalInput").ap()
    out = nc.dram_tensor("out", [T, D], F32, kind="ExternalOutput").ap()

    kscr = nc.dram_tensor("kscr", [NCH, DT, 128, 512], BF16, kind="Internal").ap()
    vscr = nc.dram_tensor("vscr", [KT, 128, D], BF16, kind="Internal").ap()
    x1scr = nc.dram_tensor("x1scr", [NQT, 128, D], F32, kind="Internal").ap()

    xin_own = x_own.rearrange("(t p) d -> t p d", p=128)
    xin_rest = x_rest.rearrange("(t p) d -> t p d", p=128)
    outv = out.rearrange("(t p) d -> t p d", p=128)

    def xtile(t):
        return xin_own[t] if t < NQT else xin_rest[t - NQT]

    wview = {nm: w.rearrange("(dt p) d -> dt p d", p=128)
             for nm, w in (("q", Wq), ("k", Wk), ("v", Wv), ("o", Wo))}
    w1v = W1.rearrange("e (dt p) h -> e dt p h", p=128)
    w2v = W2.rearrange("e (dt p) h -> e dt p h", p=128)
    w3v = W3.rearrange("e (ht p) d -> e ht p d", p=128)

    stack = contextlib.ExitStack()
    sb = lambda name, shape, dt: stack.enter_context(nc.sbuf_tensor(name, shape, dt))
    sem = lambda name: stack.enter_context(nc.semaphore(name))

    psF = stack.enter_context(nc.psum_tensor("psF", [128, 3072], F32))

    def fb(bank, cols=512, off=0, p0=0, pn=128):
        return psF[p0:p0 + pn, bank * 512 + off: bank * 512 + off + cols]

    def fbank6(par):
        return (psB6 if par == 0 else psB7)[:, :]

    sXs = [sem(f"sX{i}") for i in range(8)]
    sKWs = [sem(f"sKW{i}") for i in range(4)]
    gVWs = [sem(f"gVW{i}") for i in range(4)]
    sKRs = [sem(f"sKR{i}") for i in range(3)]
    sVRs = [sem(f"sVR{i}") for i in range(3)]
    gVRs = [sem(f"gVR{i}") for i in range(2)]
    sXOs = [sem(f"sXO{i}") for i in range(2)]
    sX1Ws = [sem(f"sX1W{i}") for i in range(2)]
    sACC, sOUT = sem("sACC"), sem("sOUT")
    gBI, gWK, gWV, gWQ, gWO, gWG = (sem("gBI"), sem("gWK"), sem("gWV"),
                                    sem("gWQ"), sem("gWO"), sem("gWG"))
    gW1s = [sem(f"gW1{i}") for i in range(2)]
    gW2s = [sem(f"gW2{i}") for i in range(2)]
    gW3s = [sem(f"gW3{i}") for i in range(2)]
    gI = sem("gI")
    tTR, tK, tV, tQ, tK2 = sem("tTR"), sem("tK"), sem("tV"), sem("tQ"), sem("tK2")
    tK2m = sem("tK2m")
    tNB, tBO, tSr = sem("tNB"), sem("tBO"), sem("tSr")
    tS, tSs, tPV, tO, tY = sem("tS"), sem("tSs"), sem("tPV"), sem("tO"), sem("tY")
    tG, tH, tO2 = sem("tG"), sem("tH"), sem("tO2")
    vA1, vA2, vA3, vA4, vK2 = sem("vA1"), sem("vA2"), sem("vA3"), sem("vA4"), sem("vK2")
    vBQ, vNB, vBO = sem("vBQ"), sem("vNB"), sem("vBO")
    vS, vXB, vX1 = sem("vS"), sem("vXB"), sem("vX1")
    vLa, vLb, vLc, vY = sem("vLa"), sem("vLb"), sem("vLc"), sem("vY")
    vG, vZ, vAcc = sem("vG"), sem("vZ"), sem("vAcc")
    vGb = sem("vGb")
    aA1, aA2, vAp = sem("aA1"), sem("aA2"), sem("vAp")
    aK, aV, aQ = sem("aK"), sem("aV"), sem("aQ")
    aNB, aNB2 = sem("aNB"), sem("aNB2")
    aB1, aB2, aLa, aLc, aB4 = sem("aB1"), sem("aB2"), sem("aLa"), sem("aLc"), sem("aB4")
    aG, aSl = sem("aG"), sem("aSl")
    bar = sem("bar")
    BAR1, BAR2 = 5, 10

    ident = sb("ident", [128, 128], BF16)
    identf = sb("identf", [128, 128], F32)
    ones1b = sb("ones1b", [1, 128], BF16)
    ones1f = sb("ones1f", [1, 128], F32)
    onescol = sb("onescol", [128, 1], BF16)
    bqs = sb("bqs", [128, DT], F32)
    bqc = sb("bqc", [128, DT], F32)
    bkc = sb("bkc", [128, DT], F32)
    bvT = sb("bvT", [128, DT], BF16)
    k2m = sb("k2m", [1, 16], F32)
    yTb = stack.enter_context(nc.sbuf_tensor("yTb", [128, DT * T], BF16, side="right"))
    epsc = sb("epsc", [128, 1], F32)
    kch = sb("kch", [128, 3 * 2048], BF16)

    GW_B = 64
    GW_K = GW_B + DT * 16
    GW_V = GW_B + 2 * DT * 16
    GW_Q = GW_B + 3 * DT * 16
    GW_O = GW_B + 4 * DT * 16

    ab = contextlib.ExitStack()
    sbAB = lambda name, shape, dt: ab.enter_context(nc.sbuf_tensor(name, shape, dt))
    qT = sbAB("qT", [128, DT * T], BF16)
    wo_b = sbAB("wo_b", [128, DT * 1024], BF16)

    aonly = contextlib.ExitStack()
    sbA = lambda name, shape, dt: aonly.enter_context(nc.sbuf_tensor(name, shape, dt))
    psB6 = aonly.enter_context(nc.psum_tensor("psB6", [128, 1024], BF16))
    psB7 = aonly.enter_context(nc.psum_tensor("psB7", [128, 1024], BF16))
    wq_b = sbA("wq_b", [128, DT * 1024], BF16)
    wk_b = sbA("wk_b", [128, DT * 1024], BF16)
    wv_b = sbA("wv_b", [128, DT * 1024], BF16)
    xb = sbA("xb", [128, 6 * 1024], F32)
    xnb = sbA("xnb", [128, 3 * 1024], BF16)
    xnT = sbA("xnT", [128, 2 * 4096], BF16)
    kst = sbA("kst", [128, 4 * 512], BF16)
    vst = sbA("vst", [128, 4 * 512], BF16)
    ksq = sbA("ksq", [128, 8 * 512], BF16)
    bnb = sbA("bnb", [128, 2 * 48], F32)
    mv = sbA("mv", [128, 2 * 8], F32)
    lnt = sbA("lnt", [128, 2 * 4], F32)
    rsb = sbA("rsb", [128, 2 * 4], F32)
    nmrs = sbA("nmrs", [128, 2 * 4], F32)
    wbuf = {"q": wq_b, "k": wk_b, "v": wv_b, "o": wo_b}

    with nc.Block("phaseA") as blk:

        @blk.gpsimd
        def _(g):
            g.memset(ident[:, :], 0.0)
            g.memset(identf[:, :], 0.0).then_inc(gI, 1)
            g.wait_ge(gI, 1)
            g.affine_select(out=ident[:, :], in_=ident[:, :],
                            compare_op=ALU.not_equal, fill=1.0, base=0,
                            pattern=[[-1, 128]], channel_multiplier=1)
            g.affine_select(out=identf[:, :], in_=identf[:, :],
                            compare_op=ALU.not_equal, fill=1.0, base=0,
                            pattern=[[-1, 128]], channel_multiplier=1).then_inc(gI, 1)
            g.memset(ones1b[:, :], 1.0)
            g.memset(ones1f[:, :], 1.0)
            g.memset(onescol[:, :], 1.0)
            g.memset(epsc[:, :], EPS).then_inc(gI, 1)
            g.dma_start(out=bqc[:, :], in_=bq).then_inc(gBI, 16)
            g.dma_start(out=bkc[:, :], in_=bk).then_inc(gBI, 16)
            g.dma_start(out=bvT[:, :], in_=bv).then_inc(gBI, 16)
            wsem = {"k": gWK, "v": gWV, "q": gWQ, "o": gWO}
            for nm in ("k", "v", "q", "o"):
                for dt in range(DT):
                    g.dma_start(out=wbuf[nm][:, dt * 1024:(dt + 1) * 1024],
                                in_=wview[nm][dt]).then_inc(wsem[nm], 16)
            for c in range(NCH):
                for u in range(8):
                    g.wait_ge(aV, 8 * c + u + 1)
                    kt, half = 4 * c + u // 2, u % 2
                    g.dma_start(out=vscr[kt][:, half * 512:(half + 1) * 512],
                                in_=vst[:, (u % 4) * 512:(u % 4 + 1) * 512]
                                ).then_inc(gVWs[u % 4], 16)
            g.wait_ge(gBI, 48)
            for s_ in (gWK, gWV, gWQ, gWO):
                g.wait_ge(s_, 16 * DT)
            for sl in range(4):
                g.wait_ge(gVWs[sl], 16 * 2 * NCH)
            g.nop().then_inc(bar, 1)

        @blk.sync
        def _(sy):
            for c in range(-2, NCH):
                if c + 2 < NCH:
                    for i in range(4):
                        t = 4 * (c + 2) + i
                        if t >= 6:
                            sy.wait_ge(vAp, t - 5)
                        sy.dma_start(out=xb[:, (t % 6) * 1024:(t % 6 + 1) * 1024],
                                     in_=xtile(t)).then_inc(sXs[t % 6], 16)
                if c >= 0:
                    for ot in range(DT):
                        sy.wait_ge(aK, 8 * c + ot + 1)
                        sy.dma_start(out=kscr[c, ot],
                                     in_=kst[:, (ot % 4) * 512:(ot % 4 + 1) * 512]
                                     ).then_inc(sKWs[ot % 4], 16)
            for sl in range(4):
                sy.wait_ge(sKWs[sl], 16 * 2)
            for hh in range(2):
                sy.dma_start(out=kch[:, hh * 2048:(hh + 1) * 2048]
                             .rearrange("q (dt n) -> q dt n", dt=DT),
                             in_=kscr[0][:, :, hh * 256:(hh + 1) * 256]
                             .rearrange("dt p n -> p dt n")
                             ).then_inc(sKRs[hh], 16)
            sy.wait_ge(sKWs[0], 16 * 4)
            sy.wait_ge(sKWs[1], 16 * 4)
            sy.wait_ge(sKWs[2], 16 * 4)
            sy.wait_ge(sKWs[3], 16 * 4)
            sy.dma_start(out=kch[:, 2 * 2048:3 * 2048]
                         .rearrange("q (dt n) -> q dt n", dt=DT),
                         in_=kscr[1][:, :, 0:256]
                         .rearrange("dt p n -> p dt n")
                         ).then_inc(sKRs[2], 16)
            for sl in range(4):
                sy.wait_ge(sKWs[sl], 16 * 2 * NCH)
            for sl in range(6):
                nsl = len([t for t in range(4 * NCH) if t % 6 == sl])
                sy.wait_ge(sXs[sl], 16 * nsl)
            sy.nop().then_inc(bar, 1)

        @blk.vector
        def _(v):
            v.wait_ge(gBI, 48)
            v.tensor_scalar(bqs[:, :], bqc[:, :], 1.0 / 32.0, None, ALU.mult
                            ).then_inc(vBQ, 1)
            def stats_chunk(c):
                p = c % 2
                for i in range(4):
                    t = 4 * c + i
                    v.wait_ge(sXs[t % 6], 16 * (t // 6 + 1))
                if c >= 2:
                    v.wait_ge(vA2, c - 1)
                for i in range(4):
                    t = 4 * c + i
                    v.bn_stats(bnb[:, p * 48 + 12 * i: p * 48 + 12 * i + 6],
                               xb[:, (t % 6) * 1024:(t % 6) * 1024 + 512])
                    ins = v.bn_stats(bnb[:, p * 48 + 12 * i + 6: p * 48 + 12 * (i + 1)],
                                     xb[:, (t % 6) * 1024 + 512:(t % 6 + 1) * 1024])
                ins.then_inc(vA1, 1)
                v.wait_ge(vA1, c + 1)
                if c >= 2:
                    v.wait_ge(vA3, c - 1)
                for i in range(4):
                    ins = v.bn_aggr(mv[:, p * 8 + 2 * i: p * 8 + 2 * i + 2],
                                    bnb[:, p * 48 + 12 * i: p * 48 + 12 * (i + 1)])
                ins.then_inc(vA2, 1)

            stats_chunk(0)
            for c in range(NCH):
                p = c % 2
                v.wait_ge(aA2, c + 1)
                for i in range(4):
                    ins = v.scalar_tensor_tensor(
                        nmrs[:, p * 4 + i: p * 4 + i + 1],
                        mv[:, p * 8 + 2 * i: p * 8 + 2 * i + 1], -1.0,
                        rsb[:, p * 4 + i: p * 4 + i + 1], ALU.mult, ALU.mult)
                ins.then_inc(vA3, 1)
                v.wait_ge(vA3, c + 1)
                for i in range(4):
                    t = 4 * c + i
                    if t - 2 > 0:
                        v.wait_ge(tTR, t - 2)
                    v.tensor_scalar(xnb[:, (t % 3) * 1024:(t % 3 + 1) * 1024],
                                    xb[:, (t % 6) * 1024:(t % 6 + 1) * 1024],
                                    rsb[:, p * 4 + i: p * 4 + i + 1],
                                    nmrs[:, p * 4 + i: p * 4 + i + 1],
                                    ALU.mult, ALU.add).then_inc(vAp, 1)
                for i in range(4):
                    t = 4 * c + i
                    v.wait_ge(tTR, t + 1)
                    v.tensor_copy(
                        xnT[:, p * 4096:(p + 1) * 4096]
                        .rearrange("q (dt n) -> q dt n", dt=DT)[:, :, i * 128:(i + 1) * 128],
                        fbank6(t % 2).rearrange("q (dt n) -> q dt n", dt=DT)
                    ).then_inc(vA4, 1)
                if c >= 1:
                    v.wait_ge(tK2m, 8 * c)
                    v.reduce_max(k2m[0:1, c - 1:c], fb(4 + ((c - 1) % 2), pn=1),
                                 axis=AX.X).then_inc(vK2, 1)
                if c + 1 < NCH:
                    stats_chunk(c + 1)
            v.wait_ge(tK2m, 8 * NCH)
            v.reduce_max(k2m[0:1, NCH - 1:NCH], fb(4 + ((NCH - 1) % 2), pn=1),
                         axis=AX.X).then_inc(vK2, 1)
            v.nop().then_inc(bar, 1)

        @blk.scalar
        def _(s):
            def evacsKV(c):
                for ot in range(DT):
                    s.wait_ge(tK, 8 * c + ot + 1)
                    if 8 * c + ot >= 4:
                        s.wait_ge(sKWs[ot % 4], 16 * ((8 * c + ot) // 4))
                    s.activation(kst[:, (ot % 4) * 512:(ot % 4 + 1) * 512],
                                 fb(ot % 2), AF.Identity, bias=bkc[:, ot:ot + 1])
                    if c >= 1:
                        s.wait_ge(tK2m, 8 * (c - 1) + ot + 1)
                    s.activation(ksq[:, ot * 512:(ot + 1) * 512],
                                 fb(ot % 2), AF.Square, bias=bkc[:, ot:ot + 1]
                                 ).then_inc(aK, 1)
                for u in range(8):
                    s.wait_ge(tV, 8 * c + u + 1)
                    if 8 * c + u >= 4:
                        s.wait_ge(gVWs[u % 4], 16 * ((8 * c + u) // 4))
                    s.activation(vst[:, (u % 4) * 512:(u % 4 + 1) * 512],
                                 fb(2 + u % 2), AF.Identity).then_inc(aV, 1)

            def evacsQ(c):
                if c < OWN_CH:
                    s.wait_ge(vBQ, 1)
                    for ot in range(DT):
                        s.wait_ge(tQ, 8 * c + ot + 1)
                        s.activation(qT[:, ot * T + 512 * c: ot * T + 512 * (c + 1)],
                                     fb(ot % 2), AF.Identity, scale=1.0 / 32.0,
                                     bias=bqs[:, ot:ot + 1]).then_inc(aQ, 1)

            s.wait_ge(gI, 3)

            def lnrs(c):
                p = c % 2
                s.wait_ge(vA2, c + 1)
                if c >= 2:
                    s.wait_ge(vAp, 4 * (c - 1))
                for i in range(4):
                    ins = s.activation(lnt[:, p * 4 + i: p * 4 + i + 1],
                                       mv[:, p * 8 + 2 * i + 1: p * 8 + 2 * i + 2],
                                       AF.Ln, bias=epsc[:, 0:1])
                ins.then_inc(aA1, 1)
                s.wait_ge(aA1, c + 1)
                s.activation(rsb[:, p * 4:(p + 1) * 4], lnt[:, p * 4:(p + 1) * 4],
                             AF.Exp, scale=-0.5).then_inc(aA2, 1)

            lnrs(0)
            for c in range(NCH):
                if c >= 1:
                    evacsKV(c - 1)
                if c + 1 < NCH:
                    lnrs(c + 1)
                if c >= 1:
                    evacsQ(c - 1)
            evacsKV(NCH - 1)
            evacsQ(NCH - 1)
            s.nop().then_inc(bar, 1)

        @blk.tensor
        def _(te):
            def k2mm(c, ot):
                te.wait_ge(aK, 8 * c + ot + 1)
                if ot == 0:
                    te.wait_ge(gI, 3)
                    if c >= 2:
                        te.wait_ge(vK2, c - 1)
                te.matmul(fb(4 + c % 2, pn=1), onescol[:, :],
                          ksq[:, ot * 512:(ot + 1) * 512],
                          start=(ot == 0), stop=(ot == DT - 1)
                          ).then_inc(tK2m, 1)

            def proj(c):
                p = c % 2
                te.wait_ge(vA4, 4 * c + 4)
                if c == 0:
                    te.wait_ge(gWK, 16 * DT)
                if 1 <= c <= OWN_CH:
                    te.wait_ge(aQ, 8 * c)
                for ot in range(DT):
                    if 8 * c + ot - 1 > 0:
                        te.wait_ge(aK, 8 * c + ot - 1)
                    for dt in range(DT):
                        ins = te.matmul(
                            fb(ot % 2),
                            wk_b[:, dt * 1024 + ot * 128: dt * 1024 + (ot + 1) * 128],
                            xnT[:, p * 4096 + dt * 512: p * 4096 + (dt + 1) * 512],
                            start=(dt == 0), stop=(dt == DT - 1))
                    ins.then_inc(tK, 1)
                    if ot >= 1:
                        k2mm(c, ot - 1)
                k2mm(c, DT - 1)
                if c == 0:
                    te.wait_ge(gWV, 16 * DT)
                for u in range(8):
                    if 8 * c + u - 1 > 0:
                        te.wait_ge(aV, 8 * c + u - 1)
                    i, half = u // 2, u % 2
                    for dt in range(DT):
                        ins = te.matmul(
                            fb(2 + u % 2),
                            xnT[:, p * 4096 + dt * 512 + i * 128:
                                p * 4096 + dt * 512 + (i + 1) * 128],
                            wv_b[:, dt * 1024 + half * 512: dt * 1024 + (half + 1) * 512],
                            start=(dt == 0), stop=(dt == DT - 1))
                    ins.then_inc(tV, 1)
                if c < OWN_CH:
                    if c == 0:
                        te.wait_ge(gWQ, 16 * DT)
                    te.wait_ge(aK, 8 * (c + 1))
                    for ot in range(DT):
                        if 8 * c + ot - 1 > 0:
                            te.wait_ge(aQ, 8 * c + ot - 1)
                        for dt in range(DT):
                            ins = te.matmul(
                                fb(ot % 2),
                                wq_b[:, dt * 1024 + ot * 128: dt * 1024 + (ot + 1) * 128],
                                xnT[:, p * 4096 + dt * 512: p * 4096 + (dt + 1) * 512],
                                start=(dt == 0), stop=(dt == DT - 1))
                        ins.then_inc(tQ, 1)

            te.wait_ge(gI, 2)
            for c in range(NCH):
                for i in range(4):
                    t = 4 * c + i
                    te.wait_ge(vAp, t + 1)
                    if t - 1 > 0:
                        te.wait_ge(vA4, t - 1)
                    for dt in range(DT):
                        ins = te.transpose(
                            fbank6(t % 2)[:, dt * 128:(dt + 1) * 128],
                            xnb[:, (t % 3) * 1024 + dt * 128:(t % 3) * 1024 + (dt + 1) * 128],
                            ident[:, :])
                    ins.then_inc(tTR, 1)
                if c >= 1:
                    proj(c - 1)
            proj(NCH - 1)
            te.nop().then_inc(bar, 1)

    aonly.close()
    ps2 = contextlib.ExitStack()
    psF2 = ps2.enter_context(nc.psum_tensor("psF2", [128, 1024], F32))
    bscope = contextlib.ExitStack()
    sbB = lambda name, shape, dt: bscope.enter_context(nc.sbuf_tensor(name, shape, dt))

    vch = sbB("vch", [128, 5 * 1024], BF16)
    bob = sbB("bob", [128, D], BF16)
    bor = sbB("bor", [1, D], F32)
    bopr = sbB("bopr", [1, D], F32)

    mk2 = sbB("mk2", [1, 8], F32)
    cb = sbB("cb", [128, 1], F32)
    srow = bopr[0:1, 0:512]
    pT = sbB("pT", [128, KT * 512], BF16)

    aT = sbB("aT", [128, 8 * 512], BF16)
    q2r = aT[0:1, 0:T]
    qsqb = aT[:, 2048:3072]
    xo = sbB("xo", [128, 2 * 1024], F32)
    x1t = sbB("x1t", [128, 2 * 1024], F32)
    yt = sbB("yt", [128, 1024], F32)
    sR = sbB("sR", [128, 4], F32)
    rR = sbB("rR", [128, 4], F32)
    bnb2 = sbB("bnb2", [128, 12], F32)
    mv2 = sbB("mv2", [128, 2 * 2], F32)
    ln2s = sbB("ln2s", [128, 2 * 4], F32)

    Q2P, Q2L, Q2G = C.Q2P, C.Q2L, C.Q2G
    NQ2 = Q2P * Q2G

    def OBANK(u2):
        return fb((4, 5, 3)[u2 % 3])

    with nc.Block("phaseB") as blk:

        @blk.gpsimd
        def _(g):
            g.wait_ge(bar, BAR1)
            g.dma_start(out=bor[:, :], in_=bo).then_inc(gBI, 16)
            for qb in range(NQB):
                for kt in range(KT):
                    gv = qb * KT + kt
                    if gv % 2 != 1:
                        continue
                    if gv - 3 > 0:
                        g.wait_ge(tPV, gv - 3)
                    sl = 3 + (gv // 2) % 2
                    g.dma_start(out=vch[:, sl * 1024:(sl + 1) * 1024],
                                in_=vscr[kt]).then_inc(gVRs[sl - 3], 16)
            g.nop().then_inc(bar, 1)

        @blk.scalar
        def _(s):
            s.wait_ge(bar, BAR1)
            for pc in range(Q2P):
                for dt in range(DT):
                    k = pc * DT + dt
                    if k > 1:
                        s.wait_ge(tNB, k - 1)
                    s.activation(qsqb[:, (k % 2) * Q2L:(k % 2 + 1) * Q2L],
                                 qT[:, dt * T + pc * Q2L: dt * T + (pc + 1) * Q2L],
                                 AF.Square).then_inc(aNB, 1)
            s.wait_ge(vNB, NQ2 + 1)
            s.activation(mk2[0:1, 2:3], mk2[0:1, 0:1], AF.Ln)
            s.activation(mk2[0:1, 3:4], mk2[0:1, 1:2], AF.Ln).then_inc(aNB2, 1)
            s.wait_ge(vNB, NQ2 + 2)
            s.activation(mk2[0:1, 5:6], mk2[0:1, 4:5], AF.Exp, scale=0.5
                         ).then_inc(aNB2, 1)

            s.wait_ge(vNB, NQ2 + 4)
            for qb in range(NQB):
                for kt in range(KT):
                    gS = qb * KT + kt
                    s.wait_ge(tS, gS + 1)
                    if kt == 0 and qb >= 1:
                        s.wait_ge(tPV, qb * KT)
                    s.activation(pT[:, kt * 512:(kt + 1) * 512],
                                 fb(gS % 4), AF.Exp, bias=cb[:, 0:1]
                                 ).then_inc(aB1, 1)
                s.wait_ge(tPV, (qb + 1) * KT)
                if qb >= 1:
                    s.wait_ge(tO, qb * 8)
                for ct in range(8):
                    sb_ = fb(ct) if ct < 6 else psF2[:, (ct - 6) * 512:(ct - 5) * 512]
                    s.activation(aT[:, ct * 512:(ct + 1) * 512],
                                 sb_, AF.Identity).then_inc(aB2, 1)
                for qs in range(4):
                    gq = qb * 4 + qs
                    pq = qs % 2
                    s.wait_ge(vLb, gq + 1)
                    s.activation(ln2s[:, pq * 4: pq * 4 + 1],
                                 mv2[:, pq * 2 + 1: pq * 2 + 2],
                                 AF.Ln, bias=epsc[:, 0:1]).then_inc(aLa, 1)
                    s.wait_ge(aLa, gq + 1)
                    s.activation(ln2s[:, pq * 4 + 1: pq * 4 + 2],
                                 ln2s[:, pq * 4: pq * 4 + 1],
                                 AF.Exp, scale=-0.5).then_inc(aLc, 1)
                    s.wait_ge(vLc, gq + 1)
                    if gq > 0:
                        s.wait_ge(tY, gq)
                    s.activation(yt[:, 0:1024],
                                 x1t[:, pq * 1024:(pq + 1) * 1024], AF.Identity,
                                 scale=ln2s[:, pq * 4 + 1: pq * 4 + 2],
                                 bias=ln2s[:, pq * 4 + 2: pq * 4 + 3]).then_inc(aB4, 1)
            s.nop().then_inc(bar, 1)

        @blk.vector
        def _(v):
            def ystev(gq, bank):
                v.wait_ge(tY, gq + 1)
                v.tensor_copy(
                    yTb.rearrange("q (dt n) -> q dt n", n=T)[:, :, gq * 128:(gq + 1) * 128],
                    psF[:, bank * 512:(bank + 2) * 512]
                    .rearrange("q (dt n) -> q dt n", dt=DT)
                ).then_inc(vY, 1)

            v.wait_ge(bar, BAR1)
            v.wait_ge(tBO, 1)
            v.wait_ge(gBI, 64)
            v.tensor_add(bopr[0:1, :], fb(4, cols=1024, pn=1), bor[0:1, :]
                         ).then_inc(vBO, 1)
            for pc in range(Q2P):
                v.wait_ge(tNB, (pc + 1) * DT)
                for j in range(Q2G):
                    v.tensor_copy(q2r[:1, pc * Q2L + j * 512: pc * Q2L + (j + 1) * 512],
                                  fb(4 + j, pn=1)).then_inc(vNB, 1)
            v.wait_ge(vNB, NQ2)
            v.wait_ge(vK2, NCH)
            v.reduce_max(mk2[0:1, 0:1], k2m[0:1, 0:NCH], axis=AX.X)
            v.reduce_max(mk2[0:1, 1:2], q2r[0:1, :], axis=AX.X).then_inc(vNB, 1)
            v.wait_ge(aNB2, 1)
            v.tensor_add(mk2[0:1, 4:5], mk2[0:1, 2:3], mk2[0:1, 3:4]
                         ).then_inc(vNB, 1)
            v.wait_ge(aNB2, 2)
            v.tensor_scalar(mk2[0:1, 6:7], mk2[0:1, 5:6], -1.0, None, ALU.mult
                            ).then_inc(vNB, 1)
            v.wait_ge(tBO, 3)
            v.tensor_copy(cb[:, :], fb(4, cols=1)).then_inc(vNB, 1)
            v.tensor_copy(bob[:, :], psF[:, 0:1024]).then_inc(vBO, 1)

            for qb in range(NQB):
                if qb >= 1:
                    ystev(qb * 4 - 1, bank=4)
                v.wait_ge(tSs, qb + 1)
                v.tensor_copy(srow[0:1, :], fb(4, pn=1)).then_inc(vS, 1)
                v.wait_ge(tSr, qb + 1)
                v.tensor_copy(sR[:, :], fb(5, cols=4)).then_inc(vS, 1)
                v.wait_ge(vS, 3 * qb + 2)
                v.reciprocal(rR[:, :], sR[:, :]).then_inc(vS, 1)
                v.wait_ge(vS, 3 * qb + 3)
                for qs in range(4):
                    gq = qb * 4 + qs
                    pq = qs % 2
                    v.wait_ge(sXOs[gq % 2], 16 * (gq // 2 + 1))
                    v.wait_ge(tO, qb * 8 + qs * 2 + 2)
                    if gq - 1 > 0:
                        v.wait_ge(aB4, gq - 1)
                    if gq >= 2:
                        v.wait_ge(sX1Ws[gq % 2], 16 * (gq // 2))
                    v.scalar_tensor_tensor(x1t[:, pq * 1024:(pq + 1) * 1024],
                                           psF[:, 2048:3072], rR[:, qs:qs + 1],
                                           xo[:, pq * 1024:(pq + 1) * 1024],
                                           ALU.mult, ALU.add).then_inc(vXB, 1)
                    v.wait_ge(vXB, gq + 1)
                    v.tensor_add(x1t[:, pq * 1024:(pq + 1) * 1024],
                                 x1t[:, pq * 1024:(pq + 1) * 1024], bob[:, :]
                                 ).then_inc(vX1, 1)
                    v.wait_ge(vX1, gq + 1)
                    if gq > 0:
                        v.wait_ge(vLb, gq)
                    v.bn_stats(bnb2[:, 0:6],
                               x1t[:, pq * 1024: pq * 1024 + 512])
                    v.bn_stats(bnb2[:, 6:12],
                               x1t[:, pq * 1024 + 512:(pq + 1) * 1024]).then_inc(vLa, 1)
                    v.wait_ge(vLa, gq + 1)
                    v.bn_aggr(mv2[:, pq * 2:(pq + 1) * 2],
                              bnb2[:, 0:12]).then_inc(vLb, 1)
                    v.wait_ge(aLc, gq + 1)
                    v.scalar_tensor_tensor(ln2s[:, pq * 4 + 2: pq * 4 + 3],
                                           mv2[:, pq * 2: pq * 2 + 1], -1.0,
                                           ln2s[:, pq * 4 + 1: pq * 4 + 2],
                                           ALU.mult, ALU.mult).then_inc(vLc, 1)
                    if qs >= 1:
                        ystev(gq - 1, bank=(0 if qs <= 2 else 4))
            ystev(NQB * 4 - 1, bank=4)
            v.nop().then_inc(bar, 1)

        @blk.sync
        def _(sy):
            sy.wait_ge(bar, BAR1)

            def kchload(qb, c, skip_first=False):
                for hh in range((1 if skip_first else 0), 2):
                    gh = (qb * NCH + c) * 2 + hh
                    if 2 * gh - 4 > 0:
                        sy.wait_ge(tS, 2 * gh - 4)
                    sy.dma_start(out=kch[:, (gh % 3) * 2048:(gh % 3 + 1) * 2048]
                                 .rearrange("q (dt n) -> q dt n", dt=DT),
                                 in_=kscr[c][:, :, hh * 256:(hh + 1) * 256]
                                 .rearrange("dt p n -> p dt n")
                                 ).then_inc(sKRs[gh % 3], 16)

            for qb in range(NQB):
                if qb == 0:
                    kchload(0, 1, skip_first=True)
                    for c in range(2, NCH):
                        kchload(qb, c)
                else:
                    for c in range(1, NCH):
                        kchload(qb, c)
                for kt in range(KT):
                    gv = qb * KT + kt
                    if gv % 2 != 0:
                        continue
                    if gv - 4 > 0:
                        sy.wait_ge(tPV, gv - 4)
                    sl = (gv // 2) % 3
                    sy.dma_start(out=vch[:, sl * 1024:(sl + 1) * 1024],
                                 in_=vscr[kt]).then_inc(sVRs[sl], 16)
                for qs in range(4):
                    gq = qb * 4 + qs
                    if gq - 1 > 0:
                        sy.wait_ge(vXB, gq - 1)
                    sy.dma_start(out=xo[:, (qs % 2) * 1024:(qs % 2 + 1) * 1024],
                                 in_=xin_own[gq]).then_inc(sXOs[gq % 2], 16)
                if qb + 1 < NQB:
                    kchload(qb + 1, 0)
                for qs in range(4):
                    gq = qb * 4 + qs
                    sy.wait_ge(vX1, gq + 1)
                    sy.dma_start(out=x1scr[gq],
                                 in_=x1t[:, (qs % 2) * 1024:(qs % 2 + 1) * 1024]
                                 ).then_inc(sX1Ws[gq % 2], 16)
            for sl in range(2):
                sy.wait_ge(sX1Ws[sl], 16 * (4 * NQB // 2))
            sy.nop().then_inc(bar, 1)

        @blk.tensor
        def _(te):
            def ytr(gq, bank):
                pq = gq % 2
                te.wait_ge(aB4, gq + 1)
                if gq > 0:
                    te.wait_ge(vY, gq)
                for dt in range(DT):
                    ins = te.transpose(
                        psF[:, bank * 512 + dt * 128: bank * 512 + (dt + 1) * 128],
                        yt[:, dt * 128:(dt + 1) * 128],
                        identf[:, :])
                ins.then_inc(tY, 1)

            te.wait_ge(bar, BAR1)
            te.wait_ge(gWO, 16 * DT)
            te.wait_ge(vK2, NCH)
            for dt in range(DT):
                for h2 in range(2):
                    ins = te.matmul(fb(4 + h2, pn=1), bvT[:, dt:dt + 1],
                                    wo_b[:, dt * 1024 + h2 * 512: dt * 1024 + (h2 + 1) * 512],
                                    start=(dt == 0), stop=(dt == DT - 1))
            ins.then_inc(tBO, 1)
            te.wait_ge(vBO, 1)
            for pc in range(Q2P):
                if pc >= 1:
                    te.wait_ge(vNB, pc * Q2G)
                for dt in range(DT):
                    k = pc * DT + dt
                    te.wait_ge(aNB, k + 1)
                    ins = te.matmul(fb(4, pn=1), onescol[:, :],
                                    qsqb[:, (k % 2) * Q2L:(k % 2 + 1) * Q2L],
                                    start=(dt == 0), stop=(dt == DT - 1))
                    ins.then_inc(tNB, 1)
            for j in range(2):
                ins = te.matmul(psF[:, j * 512:(j + 1) * 512], ones1f[:, :],
                                bopr[0:1, j * 512:(j + 1) * 512],
                                start=True, stop=True)
            ins.then_inc(tBO, 1)
            te.wait_ge(vNB, NQ2 + 3)
            te.matmul(fb(4, cols=1), ones1f[:, :], mk2[0:1, 6:7],
                      start=True, stop=True).then_inc(tBO, 1)
            te.wait_ge(vNB, NQ2 + 4)
            te.wait_ge(vBO, 2)

            SROW0 = min(9, KT - 1)
            for qb in range(NQB):
                if qb >= 1:
                    te.wait_ge(aB2, qb * 8)
                if qb >= 1:
                    te.wait_ge(vY, qb * 4 - 2)
                for kt in range(KT):
                    if qb >= 1 and kt in (2, 3):
                        if kt == 2:
                            te.wait_ge(vX1, qb * 4)
                        ytr(qb * 4 - 2 + (kt - 2), bank=4)
                    c = kt // 4
                    gS = qb * KT + kt
                    gh = (qb * NCH + c) * 2 + (kt % 4) // 2
                    if kt % 2 == 0:
                        te.wait_ge(sKRs[gh % 3], 16 * (gh // 3 + 1))
                    if gS - 3 > 0:
                        te.wait_ge(aB1, gS - 3)
                    for dt in range(DT):
                        ins = te.matmul(fb(gS % 4),
                                        kch[:, (gh % 3) * 2048 + dt * 256 + (kt % 2) * 128:
                                            (gh % 3) * 2048 + dt * 256 + (kt % 2 + 1) * 128],
                                        qT[:, dt * T + qb * 512: dt * T + (qb + 1) * 512],
                                        start=(dt == 0), stop=(dt == DT - 1))
                    ins.then_inc(tS, 1)
                    if kt >= SROW0:
                        if kt == SROW0 and qb >= 1:
                            te.wait_ge(vX1, qb * 4)
                            te.wait_ge(vY, qb * 4)
                        te.wait_ge(aB1, qb * KT + kt)
                        te.matmul(fb(4, pn=1), onescol[:, :],
                                  pT[:, (kt - 1) * 512: kt * 512],
                                  start=(kt == SROW0), stop=False)
                te.wait_ge(aB1, (qb + 1) * KT)
                te.matmul(fb(4, pn=1), onescol[:, :],
                          pT[:, (KT - 1) * 512: KT * 512],
                          start=False, stop=False)
                for j in range(SROW0 - 1):
                    ins = te.matmul(fb(4, pn=1), onescol[:, :],
                                    pT[:, j * 512:(j + 1) * 512],
                                    start=False, stop=(j == SROW0 - 2))
                ins.then_inc(tSs, 1)
                te.wait_ge(vS, 3 * qb + 1)
                for qs in range(4):
                    ins = te.matmul(fb(5, cols=1, off=qs),
                                    srow[0:1, qs * 128:(qs + 1) * 128],
                                    ones1f[0:1, 0:1], start=True, stop=True)
                ins.then_inc(tSr, 1)
                te.wait_ge(vS, 3 * qb + 2)
                for kt in range(KT):
                    gv = qb * KT + kt
                    if gv % 2 == 0:
                        sl = (gv // 2) % 3
                        te.wait_ge(sVRs[sl], 16 * (gv // 6 + 1))
                    else:
                        sl = 3 + (gv // 2) % 2
                        te.wait_ge(gVRs[sl - 3], 16 * (gv // 4 + 1))
                    for ct in range(8):
                        ob = fb(ct) if ct < 6 else psF2[:, (ct - 6) * 512:(ct - 5) * 512]
                        ins = te.matmul(
                            ob,
                            vch[:, sl * 1024 + ct * 128: sl * 1024 + (ct + 1) * 128],
                            pT[:, kt * 512:(kt + 1) * 512],
                            start=(kt == 0), stop=(kt == KT - 1))
                    ins.then_inc(tPV, 1)
                te.wait_ge(aB2, qb * 8 + 8)
                te.wait_ge(vS, 3 * qb + 2)
                for qs in range(4):
                    if qs == 0:
                        if qb >= 1:
                            te.wait_ge(vX1, qb * 4)
                    else:
                        te.wait_ge(vX1, qb * 4 + qs)
                    for ch in range(2):
                        for ct in range(DT):
                            ins = te.matmul(
                                fb(4 + ch),
                                aT[:, ct * 512 + qs * 128: ct * 512 + (qs + 1) * 128],
                                wo_b[:, ct * 1024 + ch * 512: ct * 1024 + (ch + 1) * 512],
                                start=(ct == 0), stop=(ct == DT - 1))
                        ins.then_inc(tO, 1)
                    if 1 <= qs <= 2:
                        ytr(qb * 4 + qs - 1, bank=0)
                if qb == NQB - 1:
                    te.wait_ge(vX1, NQB * 4)
                    ytr(qb * 4 + 2, bank=4)
                    ytr(qb * 4 + 3, bank=4)
            te.nop().then_inc(bar, 1)

    bscope.close()
    ab.close()

    cscope = contextlib.ExitStack()
    sbC = lambda name, shape, dt: cscope.enter_context(nc.sbuf_tensor(name, shape, dt))
    acc = sbC("acc", [128, NQT * 1024], F32)
    zb = sbC("zb", [128, HC * T], BF16)
    HH = H // NSPE
    w1b = sbC("w1b", [128, 2 * DT * HH], BF16)
    w2b = sbC("w2b", [128, 2 * DT * HH], BF16)
    w3b = sbC("w3b", [128, 2 * HC * 1024], BF16)
    silu_st = sbC("silu_st", [128, 2 * 512], BF16)
    wgb = sbC("wgb", [128, DT * E], BF16)
    comb = sbC("comb", [128, NQT * E], F32)
    gm1 = sbC("gm1", [128, NQT], F32)
    gm2 = sbC("gm2", [128, NQT], F32)
    gd = sbC("gd", [128, NQT], F32)
    gw1 = sbC("gw1", [128, NQT], F32)
    gw2 = sbC("gw2", [128, NQT], F32)
    geq = sbC("geq", [128, NQT * E], F32)
    geq2 = sbC("geq2", [128, NQT * E], F32)
    gmsk = sbC("gmsk", [128, NQT * E], F32)


    with nc.Block("phaseC") as blk:

        @blk.gpsimd
        def _(g):
            g.wait_ge(bar, BAR2)
            g.dma_start(out=wgb[:, :], in_=Wg).then_inc(gWG, 16)
            for dt in range(DT):
                g.dma_start(out=w1b[:, dt * HH:(dt + 1) * HH],
                            in_=w1v[0, dt][:, 0:HH]).then_inc(gW1s[0], 16)
            for dt in range(DT):
                g.dma_start(out=w2b[:, dt * HH:(dt + 1) * HH],
                            in_=w2v[0, dt][:, 0:HH]).then_inc(gW2s[0], 16)
            for e in range(E):
                for sh in range(NSPE):
                    sub = NSPE * e + sh
                    if sub >= 1:
                        if sub >= 2:
                            g.wait_ge(tH, (sub - 1) * UH1)
                        for dt in range(DT):
                            g.dma_start(out=w1b[:, (sub % 2) * DT * HH + dt * HH:
                                               (sub % 2) * DT * HH + (dt + 1) * HH],
                                        in_=w1v[e, dt][:, sh * HH:(sh + 1) * HH]
                                        ).then_inc(gW1s[sub % 2], 16)
                        for dt in range(DT):
                            g.dma_start(out=w2b[:, (sub % 2) * DT * HH + dt * HH:
                                               (sub % 2) * DT * HH + (dt + 1) * HH],
                                        in_=w2v[e, dt][:, sh * HH:(sh + 1) * HH]
                                        ).then_inc(gW2s[sub % 2], 16)
                    if sub >= 2:
                        g.wait_ge(tO2, (sub - 1) * UO2)
                    for ht in range(HC):
                        g.dma_start(out=w3b[:, (sub % 2) * HC * 1024 + ht * 1024:
                                           (sub % 2) * HC * 1024 + (ht + 1) * 1024],
                                    in_=w3v[e, sh * HC + ht]).then_inc(gW3s[sub % 2], 16)
            g.wait_ge(gWG, 16)
            for sl in range(2):
                g.wait_ge(gW1s[sl], 16 * DT * (NSUB // 2))
                g.wait_ge(gW2s[sl], 16 * DT * (NSUB // 2))
                g.wait_ge(gW3s[sl], 16 * HC * (NSUB // 2))
            g.nop().then_inc(bar, 1)

        @blk.sync
        def _(sy):
            sy.wait_ge(bar, BAR2)
            npiece = max(1, NQT // 4)
            for j in range(NQT // npiece):
                sy.dma_start(out=acc[:, j * npiece * 1024:(j + 1) * npiece * 1024]
                             .rearrange("p (t n) -> p t n", n=1024),
                             in_=x1scr[j * npiece:(j + 1) * npiece]
                             .rearrange("t p n -> p t n")).then_inc(sACC, 16)
            for tt in range(NQT):
                sy.wait_ge(vAcc, (NSUB - 1) * UO2 + tt * 2 + 2)
                sy.dma_start(out=outv[tt], in_=acc[:, tt * 1024:(tt + 1) * 1024]
                             ).then_inc(sOUT, 16)
            sy.wait_ge(sOUT, 16 * NQT)

        @blk.tensor
        def _(te):
            te.wait_ge(bar, BAR2)
            te.wait_ge(gWG, 16)
            for tt in range(NQT):
                for dt in range(DT):
                    ins = te.matmul(fb(4, cols=E, off=E * tt),
                                    yTb[:, dt * T + tt * 128: dt * T + (tt + 1) * 128],
                                    wgb[:, dt * E:(dt + 1) * E],
                                    start=(dt == 0), stop=(dt == DT - 1))
                ins.then_inc(tG, 1)
            for e in range(E):
                for sh in range(NSPE):
                    sub = NSPE * e + sh
                    te.wait_ge(gW1s[sub % 2], 16 * DT * (sub // 2 + 1))
                    te.wait_ge(gW2s[sub % 2], 16 * DT * (sub // 2 + 1))
                    if sub >= 1:
                        last3 = UO2 - 1 - ((UO2 - 3) % 3)
                        te.wait_ge(vAcc, (sub - 1) * UO2 + last3 + 1)
                    for u in range(UH1):
                        tq, ht = divmod(u, HC)
                        gu = sub * UH1 + u
                        if gu - 2 >= 0:
                            te.wait_ge(aSl, gu - 1)
                            te.wait_ge(vZ, gu - 1)
                        for dt in range(DT):
                            te.matmul(
                                fb(2 * (u % 2)),
                                w1b[:, (sub % 2) * DT * HH + dt * HH + ht * 128:
                                    (sub % 2) * DT * HH + dt * HH + (ht + 1) * 128],
                                yTb[:, dt * T + tq * 512: dt * T + (tq + 1) * 512],
                                start=(dt == 0), stop=(dt == DT - 1))
                        for dt in range(DT):
                            ins = te.matmul(
                                fb(2 * (u % 2) + 1),
                                w2b[:, (sub % 2) * DT * HH + dt * HH + ht * 128:
                                    (sub % 2) * DT * HH + dt * HH + (ht + 1) * 128],
                                yTb[:, dt * T + tq * 512: dt * T + (tq + 1) * 512],
                                start=(dt == 0), stop=(dt == DT - 1))
                        ins.then_inc(tH, 1)
                    te.wait_ge(gW3s[sub % 2], 16 * HC * (sub // 2 + 1))
                    if sub == 0:
                        te.wait_ge(vG, 9)
                    for u2 in range(UO2):
                        tt, ch = divmod(u2, 2)
                        gu2 = sub * UO2 + u2
                        if ch == 0 and tt % 4 == 0:
                            te.wait_ge(vZ, sub * UH1 + (tt // 4 + 1) * HC)
                        if u2 == 2:
                            te.wait_ge(vZ, (sub + 1) * UH1)
                        if gu2 - 3 >= 0:
                            te.wait_ge(vAcc, gu2 - 2)
                        for ht in range(HC):
                            ins = te.matmul(
                                OBANK(u2),
                                zb[:, ht * T + tt * 128: ht * T + (tt + 1) * 128],
                                w3b[:, (sub % 2) * HC * 1024 + ht * 1024 + ch * 512:
                                    (sub % 2) * HC * 1024 + ht * 1024 + (ch + 1) * 512],
                                start=(ht == 0), stop=(ht == HC - 1))
                        ins.then_inc(tO2, 1)
            te.nop().then_inc(bar, 1)

        @blk.scalar
        def _(s):
            s.wait_ge(bar, BAR2)
            s.wait_ge(vG, 5)
            s.activation(gd[:, :], gd[:, :], AF.Exp, scale=-1.0).then_inc(aG, 1)
            for sub in range(NSUB):
                for u in range(UH1):
                    gu = sub * UH1 + u
                    s.wait_ge(tH, gu + 1)
                    if gu - 2 >= 0:
                        s.wait_ge(vZ, gu - 1)
                    s.activation(silu_st[:, (u % 2) * 512:(u % 2 + 1) * 512],
                                 fb(2 * (u % 2)), AF.Silu).then_inc(aSl, 1)
            s.nop().then_inc(bar, 1)

        @blk.vector
        def _(v):
            v.wait_ge(bar, BAR2)
            v.wait_ge(tG, NQT)
            for tt in range(NQT):
                ins = v.reduce_max(gm1[:, tt:tt + 1], fb(4, cols=E, off=E * tt),
                                   axis=AX.X)
            ins.then_inc(vG, 1)
            v.wait_ge(vG, 1)
            for tt in range(NQT):
                ins = v.tensor_single_scalar(geq[:, E * tt:E * (tt + 1)],
                                             fb(4, cols=E, off=E * tt),
                                             gm1[:, tt:tt + 1], ALU.is_equal)
            ins.then_inc(vGb, 1)
            v.wait_ge(vGb, 1)
            for tt in range(NQT):
                ins = v.scalar_tensor_tensor(gmsk[:, E * tt:E * (tt + 1)],
                                             geq[:, E * tt:E * (tt + 1)], -1e30,
                                             fb(4, cols=E, off=E * tt),
                                             ALU.mult, ALU.add)
            ins.then_inc(vG, 1)
            v.wait_ge(vG, 2)
            for tt in range(NQT):
                ins = v.reduce_max(gm2[:, tt:tt + 1], gmsk[:, E * tt:E * (tt + 1)],
                                   axis=AX.X)
            ins.then_inc(vG, 1)
            v.wait_ge(vG, 3)
            for tt in range(NQT):
                ins = v.tensor_single_scalar(geq2[:, E * tt:E * (tt + 1)],
                                             gmsk[:, E * tt:E * (tt + 1)],
                                             gm2[:, tt:tt + 1], ALU.is_equal)
            ins.then_inc(vG, 1)
            v.wait_ge(vG, 4)
            v.tensor_sub(gd[:, :], gm1[:, :], gm2[:, :]).then_inc(vG, 1)
            v.wait_ge(aG, 1)
            v.tensor_scalar(gw1[:, :], gd[:, :], 1.0, None, ALU.add).then_inc(vG, 1)
            v.wait_ge(vG, 6)
            v.reciprocal(gw1[:, :], gw1[:, :]).then_inc(vG, 1)
            v.wait_ge(vG, 7)
            v.tensor_scalar(gw2[:, :], gw1[:, :], -1.0, 1.0, ALU.mult, ALU.add
                            ).then_inc(vG, 1)
            v.wait_ge(vG, 8)
            for tt in range(NQT):
                ins = v.tensor_scalar(comb[:, E * tt:E * (tt + 1)],
                                      geq[:, E * tt:E * (tt + 1)],
                                      gw1[:, tt:tt + 1], None, ALU.mult)
            ins.then_inc(vGb, 1)
            v.wait_ge(vGb, 2)
            for tt in range(NQT):
                ins = v.scalar_tensor_tensor(comb[:, E * tt:E * (tt + 1)],
                                             geq2[:, E * tt:E * (tt + 1)],
                                             gw2[:, tt:tt + 1],
                                             comb[:, E * tt:E * (tt + 1)],
                                             ALU.mult, ALU.add)
            ins.then_inc(vG, 1)
            npiece = max(1, NQT // 4)
            v.wait_ge(sACC, 16 * (NQT // npiece))
            for sub in range(NSUB):
                e = sub // NSPE
                for u in range(UH1):
                    tq, ht = divmod(u, HC)
                    gu = sub * UH1 + u
                    v.wait_ge(aSl, gu + 1)
                    if sub >= 1 and u == 0:
                        v.wait_ge(tO2, sub * UO2)
                    v.tensor_tensor(zb[:, ht * T + tq * 512: ht * T + (tq + 1) * 512],
                                    silu_st[:, (u % 2) * 512:(u % 2 + 1) * 512],
                                    fb(2 * (u % 2) + 1), ALU.mult).then_inc(vZ, 1)
                for u2 in range(UO2):
                    tt, ch = divmod(u2, 2)
                    gu2 = sub * UO2 + u2
                    v.wait_ge(tO2, gu2 + 1)
                    v.scalar_tensor_tensor(
                        acc[:, tt * 1024 + ch * 512: tt * 1024 + (ch + 1) * 512],
                        OBANK(u2),
                        comb[:, tt * E + e: tt * E + e + 1],
                        acc[:, tt * 1024 + ch * 512: tt * 1024 + (ch + 1) * 512],
                        ALU.mult, ALU.add).then_inc(vAcc, 1)
            v.nop().then_inc(bar, 1)

    cscope.close()
    ps2.close()
    stack.close()
    return nc


_CACHE = {}


def _get_nc():
    if "full" not in _CACHE:
        _CACHE["full"] = build(Cfg())
    return _CACHE["full"]


def _shard(x, Wq, bq_, Wk, bk_, Wv, bv_, Wo, bo_, Wg_, W1, W2, W3, e):
    B, TT, _ = x.shape
    TO = TT // 4

    def bias_layout(b):
        return np.ascontiguousarray(np.asarray(b, np.float32).reshape(DT, 128).T)

    wg = np.asarray(Wg_, np.float32)
    wg_dev = np.ascontiguousarray(
        wg.reshape(DT, 128, e).transpose(1, 0, 2).reshape(128, DT * e))
    common = {
        "Wq": np.ascontiguousarray(np.asarray(Wq, np.float32)),
        "Wk": np.ascontiguousarray(np.asarray(Wk, np.float32)),
        "Wv": np.ascontiguousarray(np.asarray(Wv, np.float32)),
        "Wo": np.ascontiguousarray(np.asarray(Wo, np.float32)),
        "bq": bias_layout(bq_), "bk": bias_layout(bk_), "bv": bias_layout(bv_),
        "bo": np.ascontiguousarray(np.asarray(bo_, np.float32).reshape(1, D)),
        "Wg": wg_dev,
        "W1": np.ascontiguousarray(np.asarray(W1, np.float32)),
        "W2": np.ascontiguousarray(np.asarray(W2, np.float32)),
        "W3": np.ascontiguousarray(np.asarray(W3, np.float32)),
    }
    in_maps = []
    for core in range(NCORES):
        b, q = core // 4, core % 4
        m = dict(common)
        m["x_own"] = np.ascontiguousarray(x[b, q * TO:(q + 1) * TO])
        m["x_rest"] = np.ascontiguousarray(
            np.concatenate([x[b, :q * TO], x[b, (q + 1) * TO:]], axis=0))
        in_maps.append(m)
    return in_maps


def kernel(L0_attn_x, L0_attn_Wq, L0_attn_bq, L0_attn_Wk, L0_attn_bk,
           L0_attn_Wv, L0_attn_bv, L0_attn_Wo, L0_attn_bo,
           L0_ffn_Wg, L0_ffn_W1, L0_ffn_W2, L0_ffn_W3):
    x = np.ascontiguousarray(np.asarray(L0_attn_x, dtype=np.float32))
    B, TT, _ = x.shape
    TO = TT // 4
    in_maps = _shard(x, L0_attn_Wq, L0_attn_bq, L0_attn_Wk, L0_attn_bk,
                     L0_attn_Wv, L0_attn_bv, L0_attn_Wo, L0_attn_bo,
                     L0_ffn_Wg, L0_ffn_W1, L0_ffn_W2, L0_ffn_W3, 8)
    nc = _get_nc()
    res = run_bass_kernel_spmd(nc, in_maps, core_ids=list(range(NCORES)))
    outp = np.empty((B, TT, D), np.float32)
    for core in range(NCORES):
        b, q = core // 4, core % 4
        outp[b, q * TO:(q + 1) * TO] = res.results[core]["out"]
    return outp


# version 25
# speedup vs baseline: 1.1969x; 1.1969x over previous
import contextlib
import numpy as np

import concourse.bass as bass
import concourse.mybir as mybir
from concourse.bass_utils import run_bass_kernel_spmd

F32 = mybir.dt.float32
BF16 = mybir.dt.bfloat16
AF = mybir.ActivationFunctionType
ALU = mybir.AluOpType
AX = mybir.AxisListType

D = 1024
DT = 8
NCORES = 8
EPS = 1e-5
LOG_INV32 = float(np.log(1.0 / 32.0))


class Cfg:
    def __init__(self, t_b=8192, t_own=2048, h=2048, e=8):
        assert t_b % 512 == 0 and t_own % 512 == 0 and h % 256 == 0
        self.t_b, self.t_own, self.h, self.e = t_b, t_own, h, e
        self.NCH = t_b // 512
        self.KT = t_b // 128
        self.NQT = t_own // 128
        self.NQB = t_own // 512
        self.OWN_CH = t_own // 512
        self.NSPE = 4 if h >= 1024 else 2
        self.HC = (h // 128) // self.NSPE
        self.NTQ = t_own // 512
        self.UH1 = self.HC * self.NTQ
        self.UO2 = self.NQT * 2
        self.NSUB = e * self.NSPE
        self.Q2P = t_own // 512
        self.Q2L = 512
        self.Q2G = 1


def build(cfg: Cfg):
    nc = bass.Bass("TRN2", target_bir_lowering=False)
    C = cfg
    T, TB, H, E = C.t_own, C.t_b, C.h, C.e
    NCH, KT, NQT, NQB, OWN_CH = C.NCH, C.KT, C.NQT, C.NQB, C.OWN_CH
    HC, NTQ, UH1, UO2, NSUB, NSPE = C.HC, C.NTQ, C.UH1, C.UO2, C.NSUB, C.NSPE

    x_own = nc.dram_tensor("x_own", [T, D], F32, kind="ExternalInput").ap()
    x_rest = nc.dram_tensor("x_rest", [TB - T, D], F32, kind="ExternalInput").ap()
    Wq = nc.dram_tensor("Wq", [D, D], F32, kind="ExternalInput").ap()
    Wk = nc.dram_tensor("Wk", [D, D], F32, kind="ExternalInput").ap()
    Wv = nc.dram_tensor("Wv", [D, D], F32, kind="ExternalInput").ap()
    Wo = nc.dram_tensor("Wo", [D, D], F32, kind="ExternalInput").ap()
    bq = nc.dram_tensor("bq", [128, DT], F32, kind="ExternalInput").ap()
    bk = nc.dram_tensor("bk", [128, DT], F32, kind="ExternalInput").ap()
    bv = nc.dram_tensor("bv", [128, DT], F32, kind="ExternalInput").ap()
    bo = nc.dram_tensor("bo", [1, D], F32, kind="ExternalInput").ap()
    Wg = nc.dram_tensor("Wg", [128, DT * E], F32, kind="ExternalInput").ap()
    W1 = nc.dram_tensor("W1", [E, D, H], F32, kind="ExternalInput").ap()
    W2 = nc.dram_tensor("W2", [E, D, H], F32, kind="ExternalInput").ap()
    W3 = nc.dram_tensor("W3", [E, H, D], F32, kind="ExternalInput").ap()
    out = nc.dram_tensor("out", [T, D], F32, kind="ExternalOutput").ap()

    kscr = nc.dram_tensor("kscr", [NCH, DT, 128, 512], BF16, kind="Internal").ap()
    vscr = nc.dram_tensor("vscr", [KT, 128, D], BF16, kind="Internal").ap()
    x1scr = nc.dram_tensor("x1scr", [NQT, 128, D], F32, kind="Internal").ap()

    xin_own = x_own.rearrange("(t p) d -> t p d", p=128)
    xin_rest = x_rest.rearrange("(t p) d -> t p d", p=128)
    outv = out.rearrange("(t p) d -> t p d", p=128)

    def xtile(t):
        return xin_own[t] if t < NQT else xin_rest[t - NQT]

    wview = {nm: w.rearrange("(dt p) d -> dt p d", p=128)
             for nm, w in (("q", Wq), ("k", Wk), ("v", Wv), ("o", Wo))}
    w1v = W1.rearrange("e (dt p) h -> e dt p h", p=128)
    w2v = W2.rearrange("e (dt p) h -> e dt p h", p=128)
    w3v = W3.rearrange("e (ht p) d -> e ht p d", p=128)

    stack = contextlib.ExitStack()
    sb = lambda name, shape, dt: stack.enter_context(nc.sbuf_tensor(name, shape, dt))
    sem = lambda name: stack.enter_context(nc.semaphore(name))

    psF = stack.enter_context(nc.psum_tensor("psF", [128, 3072], F32))

    def fb(bank, cols=512, off=0, p0=0, pn=128):
        return psF[p0:p0 + pn, bank * 512 + off: bank * 512 + off + cols]

    def fbank6(par):
        return (psB6 if par == 0 else psB7)[:, :]

    sXs = [sem(f"sX{i}") for i in range(8)]
    sKWs = [sem(f"sKW{i}") for i in range(4)]
    gVWs = [sem(f"gVW{i}") for i in range(4)]
    sKRs = [sem(f"sKR{i}") for i in range(3)]
    sVRs = [sem(f"sVR{i}") for i in range(3)]
    gVRs = [sem(f"gVR{i}") for i in range(2)]
    sXOs = [sem(f"sXO{i}") for i in range(2)]
    sX1Ws = [sem(f"sX1W{i}") for i in range(2)]
    sACC, sOUT = sem("sACC"), sem("sOUT")
    gBI, gWK, gWV, gWQ, gWO, gWG = (sem("gBI"), sem("gWK"), sem("gWV"),
                                    sem("gWQ"), sem("gWO"), sem("gWG"))
    gW1s = [sem(f"gW1{i}") for i in range(2)]
    gW2s = [sem(f"gW2{i}") for i in range(2)]
    gW3s = [sem(f"gW3{i}") for i in range(2)]
    gI = sem("gI")
    tTR, tK, tV, tQ, tK2 = sem("tTR"), sem("tK"), sem("tV"), sem("tQ"), sem("tK2")
    tK2m = sem("tK2m")
    tNB, tBO, tSr = sem("tNB"), sem("tBO"), sem("tSr")
    tS, tSs, tPV, tO, tY = sem("tS"), sem("tSs"), sem("tPV"), sem("tO"), sem("tY")
    tG, tH, tO2 = sem("tG"), sem("tH"), sem("tO2")
    vA1, vA2, vA3, vA4, vK2 = sem("vA1"), sem("vA2"), sem("vA3"), sem("vA4"), sem("vK2")
    vBQ, vNB, vBO = sem("vBQ"), sem("vNB"), sem("vBO")
    vS, vXB, vX1 = sem("vS"), sem("vXB"), sem("vX1")
    vLa, vLb, vLc, vY = sem("vLa"), sem("vLb"), sem("vLc"), sem("vY")
    vG, vZ, vAcc = sem("vG"), sem("vZ"), sem("vAcc")
    vGb = sem("vGb")
    aA1, aA2, vAp = sem("aA1"), sem("aA2"), sem("vAp")
    aK, aV, aQ = sem("aK"), sem("aV"), sem("aQ")
    aNB, aNB2, aXT = sem("aNB"), sem("aNB2"), sem("aXT")
    aB1, aB2, aLa, aLc, aB4 = sem("aB1"), sem("aB2"), sem("aLa"), sem("aLc"), sem("aB4")
    aG, aSl = sem("aG"), sem("aSl")
    bar = sem("bar")
    BAR1, BAR2 = 5, 10

    ident = sb("ident", [128, 128], BF16)
    identf = sb("identf", [128, 128], F32)
    ones1b = sb("ones1b", [1, 128], BF16)
    ones1f = sb("ones1f", [1, 128], F32)
    onescol = sb("onescol", [128, 1], BF16)
    bqs = sb("bqs", [128, DT], F32)
    bqc = sb("bqc", [128, DT], F32)
    bkc = sb("bkc", [128, DT], F32)
    bvT = sb("bvT", [128, DT], BF16)
    k2m = sb("k2m", [1, 16], F32)
    yTb = stack.enter_context(nc.sbuf_tensor("yTb", [128, DT * T], BF16, side="right"))
    epsc = sb("epsc", [128, 1], F32)
    kch = sb("kch", [128, 3 * 2048], BF16)

    GW_B = 64
    GW_K = GW_B + DT * 16
    GW_V = GW_B + 2 * DT * 16
    GW_Q = GW_B + 3 * DT * 16
    GW_O = GW_B + 4 * DT * 16

    ab = contextlib.ExitStack()
    sbAB = lambda name, shape, dt: ab.enter_context(nc.sbuf_tensor(name, shape, dt))
    qT = sbAB("qT", [128, DT * T], BF16)
    wo_b = sbAB("wo_b", [128, DT * 1024], BF16)

    aonly = contextlib.ExitStack()
    sbA = lambda name, shape, dt: aonly.enter_context(nc.sbuf_tensor(name, shape, dt))
    psB6 = aonly.enter_context(nc.psum_tensor("psB6", [128, 1024], BF16))
    psB7 = aonly.enter_context(nc.psum_tensor("psB7", [128, 1024], BF16))
    wq_b = sbA("wq_b", [128, DT * 1024], BF16)
    wk_b = sbA("wk_b", [128, DT * 1024], BF16)
    wv_b = sbA("wv_b", [128, DT * 1024], BF16)
    xb = sbA("xb", [128, 6 * 1024], F32)
    xnb = sbA("xnb", [128, 3 * 1024], BF16)
    xnT = sbA("xnT", [128, 2 * 4096], BF16)
    kst = sbA("kst", [128, 4 * 512], BF16)
    vst = sbA("vst", [128, 4 * 512], BF16)
    ksq = sbA("ksq", [128, 8 * 512], BF16)
    bnb = sbA("bnb", [128, 2 * 48], F32)
    mv = sbA("mv", [128, 2 * 8], F32)
    lnt = sbA("lnt", [128, 2 * 4], F32)
    rsb = sbA("rsb", [128, 2 * 4], F32)
    nmrs = sbA("nmrs", [128, 2 * 4], F32)
    wbuf = {"q": wq_b, "k": wk_b, "v": wv_b, "o": wo_b}

    with nc.Block("phaseA") as blk:

        @blk.gpsimd
        def _(g):
            g.memset(ident[:, :], 0.0)
            g.memset(identf[:, :], 0.0).then_inc(gI, 1)
            g.wait_ge(gI, 1)
            g.affine_select(out=ident[:, :], in_=ident[:, :],
                            compare_op=ALU.not_equal, fill=1.0, base=0,
                            pattern=[[-1, 128]], channel_multiplier=1)
            g.affine_select(out=identf[:, :], in_=identf[:, :],
                            compare_op=ALU.not_equal, fill=1.0, base=0,
                            pattern=[[-1, 128]], channel_multiplier=1).then_inc(gI, 1)
            g.memset(ones1b[:, :], 1.0)
            g.memset(ones1f[:, :], 1.0)
            g.memset(onescol[:, :], 1.0)
            g.memset(epsc[:, :], EPS).then_inc(gI, 1)
            g.dma_start(out=bqc[:, :], in_=bq).then_inc(gBI, 16)
            g.dma_start(out=bkc[:, :], in_=bk).then_inc(gBI, 16)
            g.dma_start(out=bvT[:, :], in_=bv).then_inc(gBI, 16)
            wsem = {"k": gWK, "v": gWV, "q": gWQ, "o": gWO}
            for nm in ("k", "v", "q", "o"):
                for dt in range(DT):
                    g.dma_start(out=wbuf[nm][:, dt * 1024:(dt + 1) * 1024],
                                in_=wview[nm][dt]).then_inc(wsem[nm], 16)
            for c in range(NCH):
                for u in range(8):
                    g.wait_ge(aV, 8 * c + u + 1)
                    kt, half = 4 * c + u // 2, u % 2
                    g.dma_start(out=vscr[kt][:, half * 512:(half + 1) * 512],
                                in_=vst[:, (u % 4) * 512:(u % 4 + 1) * 512]
                                ).then_inc(gVWs[u % 4], 16)
            g.wait_ge(gBI, 48)
            for s_ in (gWK, gWV, gWQ, gWO):
                g.wait_ge(s_, 16 * DT)
            for sl in range(4):
                g.wait_ge(gVWs[sl], 16 * 2 * NCH)
            g.nop().then_inc(bar, 1)

        @blk.sync
        def _(sy):
            for c in range(-2, NCH):
                if c + 2 < NCH:
                    for i in range(4):
                        t = 4 * (c + 2) + i
                        if t >= 6:
                            sy.wait_ge(vAp, t - 5)
                        sy.dma_start(out=xb[:, (t % 6) * 1024:(t % 6 + 1) * 1024],
                                     in_=xtile(t)).then_inc(sXs[t % 6], 16)
                if c >= 0:
                    for ot in range(DT):
                        sy.wait_ge(aK, 8 * c + ot + 1)
                        sy.dma_start(out=kscr[c, ot],
                                     in_=kst[:, (ot % 4) * 512:(ot % 4 + 1) * 512]
                                     ).then_inc(sKWs[ot % 4], 16)
            for sl in range(4):
                sy.wait_ge(sKWs[sl], 16 * 2)
            for hh in range(2):
                sy.dma_start(out=kch[:, hh * 2048:(hh + 1) * 2048]
                             .rearrange("q (dt n) -> q dt n", dt=DT),
                             in_=kscr[0][:, :, hh * 256:(hh + 1) * 256]
                             .rearrange("dt p n -> p dt n")
                             ).then_inc(sKRs[hh], 16)
            sy.wait_ge(sKWs[0], 16 * 4)
            sy.wait_ge(sKWs[1], 16 * 4)
            sy.wait_ge(sKWs[2], 16 * 4)
            sy.wait_ge(sKWs[3], 16 * 4)
            sy.dma_start(out=kch[:, 2 * 2048:3 * 2048]
                         .rearrange("q (dt n) -> q dt n", dt=DT),
                         in_=kscr[1][:, :, 0:256]
                         .rearrange("dt p n -> p dt n")
                         ).then_inc(sKRs[2], 16)
            for sl in range(4):
                sy.wait_ge(sKWs[sl], 16 * 2 * NCH)
            for sl in range(6):
                nsl = len([t for t in range(4 * NCH) if t % 6 == sl])
                sy.wait_ge(sXs[sl], 16 * nsl)
            sy.nop().then_inc(bar, 1)

        @blk.vector
        def _(v):
            v.wait_ge(gBI, 48)
            v.tensor_scalar(bqs[:, :], bqc[:, :], 1.0 / 32.0, None, ALU.mult
                            ).then_inc(vBQ, 1)
            def stats_chunk(c):
                p = c % 2
                for i in range(4):
                    t = 4 * c + i
                    v.wait_ge(sXs[t % 6], 16 * (t // 6 + 1))
                if c >= 2:
                    v.wait_ge(vA2, c - 1)
                for i in range(4):
                    t = 4 * c + i
                    v.bn_stats(bnb[:, p * 48 + 12 * i: p * 48 + 12 * i + 6],
                               xb[:, (t % 6) * 1024:(t % 6) * 1024 + 512])
                    ins = v.bn_stats(bnb[:, p * 48 + 12 * i + 6: p * 48 + 12 * (i + 1)],
                                     xb[:, (t % 6) * 1024 + 512:(t % 6 + 1) * 1024])
                ins.then_inc(vA1, 1)
                v.wait_ge(vA1, c + 1)
                if c >= 2:
                    v.wait_ge(vA3, c - 1)
                for i in range(4):
                    ins = v.bn_aggr(mv[:, p * 8 + 2 * i: p * 8 + 2 * i + 2],
                                    bnb[:, p * 48 + 12 * i: p * 48 + 12 * (i + 1)])
                ins.then_inc(vA2, 1)

            stats_chunk(0)
            for c in range(NCH):
                p = c % 2
                v.wait_ge(aA2, c + 1)
                for i in range(4):
                    ins = v.scalar_tensor_tensor(
                        nmrs[:, p * 4 + i: p * 4 + i + 1],
                        mv[:, p * 8 + 2 * i: p * 8 + 2 * i + 1], -1.0,
                        rsb[:, p * 4 + i: p * 4 + i + 1], ALU.mult, ALU.mult)
                ins.then_inc(vA3, 1)
                v.wait_ge(vA3, c + 1)
                for i in range(4):
                    t = 4 * c + i
                    if t - 2 > 0:
                        v.wait_ge(tTR, t - 2)
                    v.tensor_scalar(xnb[:, (t % 3) * 1024:(t % 3 + 1) * 1024],
                                    xb[:, (t % 6) * 1024:(t % 6 + 1) * 1024],
                                    rsb[:, p * 4 + i: p * 4 + i + 1],
                                    nmrs[:, p * 4 + i: p * 4 + i + 1],
                                    ALU.mult, ALU.add).then_inc(vAp, 1)
                if c >= 1:
                    v.wait_ge(tK2m, 8 * c)
                    v.reduce_max(k2m[0:1, c - 1:c], fb(4 + ((c - 1) % 2), pn=1),
                                 axis=AX.X).then_inc(vK2, 1)
                if c + 1 < NCH:
                    stats_chunk(c + 1)
            v.wait_ge(tK2m, 8 * NCH)
            v.reduce_max(k2m[0:1, NCH - 1:NCH], fb(4 + ((NCH - 1) % 2), pn=1),
                         axis=AX.X).then_inc(vK2, 1)
            v.nop().then_inc(bar, 1)

        @blk.scalar
        def _(s):
            def evacsKV(c):
                for ot in range(DT):
                    s.wait_ge(tK, 8 * c + ot + 1)
                    if 8 * c + ot >= 4:
                        s.wait_ge(sKWs[ot % 4], 16 * ((8 * c + ot) // 4))
                    s.activation(kst[:, (ot % 4) * 512:(ot % 4 + 1) * 512],
                                 fb(ot % 2), AF.Identity, bias=bkc[:, ot:ot + 1])
                    if c >= 1:
                        s.wait_ge(tK2m, 8 * (c - 1) + ot + 1)
                    s.activation(ksq[:, ot * 512:(ot + 1) * 512],
                                 fb(ot % 2), AF.Square, bias=bkc[:, ot:ot + 1]
                                 ).then_inc(aK, 1)
                for u in range(8):
                    s.wait_ge(tV, 8 * c + u + 1)
                    if 8 * c + u >= 4:
                        s.wait_ge(gVWs[u % 4], 16 * ((8 * c + u) // 4))
                    s.activation(vst[:, (u % 4) * 512:(u % 4 + 1) * 512],
                                 fb(2 + u % 2), AF.Identity).then_inc(aV, 1)

            def evacsQ(c):
                if c < OWN_CH:
                    s.wait_ge(vBQ, 1)
                    for ot in range(DT):
                        s.wait_ge(tQ, 8 * c + ot + 1)
                        s.activation(qT[:, ot * T + 512 * c: ot * T + 512 * (c + 1)],
                                     fb(ot % 2), AF.Identity, scale=1.0 / 32.0,
                                     bias=bqs[:, ot:ot + 1]).then_inc(aQ, 1)

            s.wait_ge(gI, 3)

            def lnrs(c):
                p = c % 2
                s.wait_ge(vA2, c + 1)
                if c >= 2:
                    s.wait_ge(vAp, 4 * (c - 1))
                for i in range(4):
                    ins = s.activation(lnt[:, p * 4 + i: p * 4 + i + 1],
                                       mv[:, p * 8 + 2 * i + 1: p * 8 + 2 * i + 2],
                                       AF.Ln, bias=epsc[:, 0:1])
                ins.then_inc(aA1, 1)
                s.wait_ge(aA1, c + 1)
                s.activation(rsb[:, p * 4:(p + 1) * 4], lnt[:, p * 4:(p + 1) * 4],
                             AF.Exp, scale=-0.5).then_inc(aA2, 1)

            lnrs(0)
            for c in range(NCH):
                p2 = c % 2
                for i in range(4):
                    t = 4 * c + i
                    s.wait_ge(tTR, t + 1)
                    s.activation(
                        xnT[:, p2 * 4096:(p2 + 1) * 4096]
                        .rearrange("q (dt n) -> q dt n", dt=DT)[:, :, i * 128:(i + 1) * 128],
                        fbank6(t % 2).rearrange("q (dt n) -> q dt n", dt=DT),
                        AF.Identity).then_inc(aXT, 1)
                if c >= 1:
                    evacsKV(c - 1)
                if c + 1 < NCH:
                    lnrs(c + 1)
                if c >= 1:
                    evacsQ(c - 1)
            evacsKV(NCH - 1)
            evacsQ(NCH - 1)
            s.nop().then_inc(bar, 1)

        @blk.tensor
        def _(te):
            def k2mm(c, ot):
                te.wait_ge(aK, 8 * c + ot + 1)
                if ot == 0:
                    te.wait_ge(gI, 3)
                    if c >= 2:
                        te.wait_ge(vK2, c - 1)
                te.matmul(fb(4 + c % 2, pn=1), onescol[:, :],
                          ksq[:, ot * 512:(ot + 1) * 512],
                          start=(ot == 0), stop=(ot == DT - 1)
                          ).then_inc(tK2m, 1)

            def proj(c):
                p = c % 2
                te.wait_ge(aXT, 4 * c + 4)
                if c == 0:
                    te.wait_ge(gWK, 16 * DT)
                if 1 <= c <= OWN_CH:
                    te.wait_ge(aQ, 8 * c)
                for ot in range(DT):
                    if 8 * c + ot - 1 > 0:
                        te.wait_ge(aK, 8 * c + ot - 1)
                    for dt in range(DT):
                        ins = te.matmul(
                            fb(ot % 2),
                            wk_b[:, dt * 1024 + ot * 128: dt * 1024 + (ot + 1) * 128],
                            xnT[:, p * 4096 + dt * 512: p * 4096 + (dt + 1) * 512],
                            start=(dt == 0), stop=(dt == DT - 1))
                    ins.then_inc(tK, 1)
                    if ot >= 1:
                        k2mm(c, ot - 1)
                k2mm(c, DT - 1)
                if c == 0:
                    te.wait_ge(gWV, 16 * DT)
                for u in range(8):
                    if 8 * c + u - 1 > 0:
                        te.wait_ge(aV, 8 * c + u - 1)
                    i, half = u // 2, u % 2
                    for dt in range(DT):
                        ins = te.matmul(
                            fb(2 + u % 2),
                            xnT[:, p * 4096 + dt * 512 + i * 128:
                                p * 4096 + dt * 512 + (i + 1) * 128],
                            wv_b[:, dt * 1024 + half * 512: dt * 1024 + (half + 1) * 512],
                            start=(dt == 0), stop=(dt == DT - 1))
                    ins.then_inc(tV, 1)
                if c < OWN_CH:
                    if c == 0:
                        te.wait_ge(gWQ, 16 * DT)
                    te.wait_ge(aK, 8 * (c + 1))
                    for ot in range(DT):
                        if 8 * c + ot - 1 > 0:
                            te.wait_ge(aQ, 8 * c + ot - 1)
                        for dt in range(DT):
                            ins = te.matmul(
                                fb(ot % 2),
                                wq_b[:, dt * 1024 + ot * 128: dt * 1024 + (ot + 1) * 128],
                                xnT[:, p * 4096 + dt * 512: p * 4096 + (dt + 1) * 512],
                                start=(dt == 0), stop=(dt == DT - 1))
                        ins.then_inc(tQ, 1)

            te.wait_ge(gI, 2)
            for c in range(NCH):
                for i in range(4):
                    t = 4 * c + i
                    te.wait_ge(vAp, t + 1)
                    if t - 1 > 0:
                        te.wait_ge(aXT, t - 1)
                    for dt in range(DT):
                        ins = te.transpose(
                            fbank6(t % 2)[:, dt * 128:(dt + 1) * 128],
                            xnb[:, (t % 3) * 1024 + dt * 128:(t % 3) * 1024 + (dt + 1) * 128],
                            ident[:, :])
                    ins.then_inc(tTR, 1)
                if c >= 1:
                    proj(c - 1)
            proj(NCH - 1)
            te.nop().then_inc(bar, 1)

    aonly.close()
    ps2 = contextlib.ExitStack()
    psF2 = ps2.enter_context(nc.psum_tensor("psF2", [128, 1024], F32))
    bscope = contextlib.ExitStack()
    sbB = lambda name, shape, dt: bscope.enter_context(nc.sbuf_tensor(name, shape, dt))

    vch = sbB("vch", [128, 5 * 1024], BF16)
    bob = sbB("bob", [128, D], BF16)
    bor = sbB("bor", [1, D], F32)
    bopr = sbB("bopr", [1, D], F32)

    mk2 = sbB("mk2", [1, 8], F32)
    cb = sbB("cb", [128, 1], F32)
    srow = bopr[0:1, 0:512]
    pT = sbB("pT", [128, KT * 512], BF16)

    aT = sbB("aT", [128, 8 * 512], BF16)
    q2r = aT[0:1, 0:T]
    qsqb = aT[:, 2048:3072]
    xo = sbB("xo", [128, 2 * 1024], F32)
    x1t = sbB("x1t", [128, 2 * 1024], F32)
    yt = sbB("yt", [128, 1024], F32)
    sR = sbB("sR", [128, 4], F32)
    rR = sbB("rR", [128, 4], F32)
    bnb2 = sbB("bnb2", [128, 12], F32)
    mv2 = sbB("mv2", [128, 2 * 2], F32)
    ln2s = sbB("ln2s", [128, 2 * 4], F32)

    Q2P, Q2L, Q2G = C.Q2P, C.Q2L, C.Q2G
    NQ2 = Q2P * Q2G

    def OBANK(u2):
        return fb((4, 5, 3)[u2 % 3])

    with nc.Block("phaseB") as blk:

        @blk.gpsimd
        def _(g):
            g.wait_ge(bar, BAR1)
            g.dma_start(out=bor[:, :], in_=bo).then_inc(gBI, 16)
            for qb in range(NQB):
                for kt in range(KT):
                    gv = qb * KT + kt
                    if gv % 2 != 1:
                        continue
                    if gv - 3 > 0:
                        g.wait_ge(tPV, gv - 3)
                    sl = 3 + (gv // 2) % 2
                    g.dma_start(out=vch[:, sl * 1024:(sl + 1) * 1024],
                                in_=vscr[kt]).then_inc(gVRs[sl - 3], 16)
            g.nop().then_inc(bar, 1)

        @blk.scalar
        def _(s):
            s.wait_ge(bar, BAR1)
            for pc in range(Q2P):
                for dt in range(DT):
                    k = pc * DT + dt
                    if k > 1:
                        s.wait_ge(tNB, k - 1)
                    s.activation(qsqb[:, (k % 2) * Q2L:(k % 2 + 1) * Q2L],
                                 qT[:, dt * T + pc * Q2L: dt * T + (pc + 1) * Q2L],
                                 AF.Square).then_inc(aNB, 1)
            s.wait_ge(vNB, NQ2 + 1)
            s.activation(mk2[0:1, 2:3], mk2[0:1, 0:1], AF.Ln)
            s.activation(mk2[0:1, 3:4], mk2[0:1, 1:2], AF.Ln).then_inc(aNB2, 1)
            s.wait_ge(vNB, NQ2 + 2)
            s.activation(mk2[0:1, 5:6], mk2[0:1, 4:5], AF.Exp, scale=0.5
                         ).then_inc(aNB2, 1)

            s.wait_ge(vNB, NQ2 + 4)
            for qb in range(NQB):
                for kt in range(KT):
                    gS = qb * KT + kt
                    s.wait_ge(tS, gS + 1)
                    if kt == 0 and qb >= 1:
                        s.wait_ge(tPV, qb * KT)
                    s.activation(pT[:, kt * 512:(kt + 1) * 512],
                                 fb(gS % 4), AF.Exp, bias=cb[:, 0:1]
                                 ).then_inc(aB1, 1)
                s.wait_ge(tPV, (qb + 1) * KT)
                if qb >= 1:
                    s.wait_ge(tO, qb * 8)
                for ct in range(8):
                    sb_ = fb(ct) if ct < 6 else psF2[:, (ct - 6) * 512:(ct - 5) * 512]
                    s.activation(aT[:, ct * 512:(ct + 1) * 512],
                                 sb_, AF.Identity).then_inc(aB2, 1)
                for qs in range(4):
                    gq = qb * 4 + qs
                    pq = qs % 2
                    s.wait_ge(vLb, gq + 1)
                    s.activation(ln2s[:, pq * 4: pq * 4 + 1],
                                 mv2[:, pq * 2 + 1: pq * 2 + 2],
                                 AF.Ln, bias=epsc[:, 0:1]).then_inc(aLa, 1)
                    s.wait_ge(aLa, gq + 1)
                    s.activation(ln2s[:, pq * 4 + 1: pq * 4 + 2],
                                 ln2s[:, pq * 4: pq * 4 + 1],
                                 AF.Exp, scale=-0.5).then_inc(aLc, 1)
                    s.wait_ge(vLc, gq + 1)
                    if gq > 0:
                        s.wait_ge(tY, gq)
                    s.activation(yt[:, 0:1024],
                                 x1t[:, pq * 1024:(pq + 1) * 1024], AF.Identity,
                                 scale=ln2s[:, pq * 4 + 1: pq * 4 + 2],
                                 bias=ln2s[:, pq * 4 + 2: pq * 4 + 3]).then_inc(aB4, 1)
            s.nop().then_inc(bar, 1)

        @blk.vector
        def _(v):
            def ystev(gq, bank):
                v.wait_ge(tY, gq + 1)
                v.tensor_copy(
                    yTb.rearrange("q (dt n) -> q dt n", n=T)[:, :, gq * 128:(gq + 1) * 128],
                    psF[:, bank * 512:(bank + 2) * 512]
                    .rearrange("q (dt n) -> q dt n", dt=DT)
                ).then_inc(vY, 1)

            v.wait_ge(bar, BAR1)
            v.wait_ge(tBO, 1)
            v.wait_ge(gBI, 64)
            v.tensor_add(bopr[0:1, :], fb(4, cols=1024, pn=1), bor[0:1, :]
                         ).then_inc(vBO, 1)
            for pc in range(Q2P):
                v.wait_ge(tNB, (pc + 1) * DT)
                for j in range(Q2G):
                    v.tensor_copy(q2r[:1, pc * Q2L + j * 512: pc * Q2L + (j + 1) * 512],
                                  fb(4 + j, pn=1)).then_inc(vNB, 1)
            v.wait_ge(vNB, NQ2)
            v.wait_ge(vK2, NCH)
            v.reduce_max(mk2[0:1, 0:1], k2m[0:1, 0:NCH], axis=AX.X)
            v.reduce_max(mk2[0:1, 1:2], q2r[0:1, :], axis=AX.X).then_inc(vNB, 1)
            v.wait_ge(aNB2, 1)
            v.tensor_add(mk2[0:1, 4:5], mk2[0:1, 2:3], mk2[0:1, 3:4]
                         ).then_inc(vNB, 1)
            v.wait_ge(aNB2, 2)
            v.tensor_scalar(mk2[0:1, 6:7], mk2[0:1, 5:6], -1.0, None, ALU.mult
                            ).then_inc(vNB, 1)
            v.wait_ge(tBO, 3)
            v.tensor_copy(cb[:, :], fb(4, cols=1)).then_inc(vNB, 1)
            v.tensor_copy(bob[:, :], psF[:, 0:1024]).then_inc(vBO, 1)

            for qb in range(NQB):
                if qb >= 1:
                    ystev(qb * 4 - 1, bank=4)
                v.wait_ge(tSs, qb + 1)
                v.tensor_copy(srow[0:1, :], fb(4, pn=1)).then_inc(vS, 1)
                v.wait_ge(tSr, qb + 1)
                v.tensor_copy(sR[:, :], fb(5, cols=4)).then_inc(vS, 1)
                v.wait_ge(vS, 3 * qb + 2)
                v.reciprocal(rR[:, :], sR[:, :]).then_inc(vS, 1)
                v.wait_ge(vS, 3 * qb + 3)
                for qs in range(4):
                    gq = qb * 4 + qs
                    pq = qs % 2
                    v.wait_ge(sXOs[gq % 2], 16 * (gq // 2 + 1))
                    v.wait_ge(tO, qb * 8 + qs * 2 + 2)
                    if gq - 1 > 0:
                        v.wait_ge(aB4, gq - 1)
                    if gq >= 2:
                        v.wait_ge(sX1Ws[gq % 2], 16 * (gq // 2))
                    v.scalar_tensor_tensor(x1t[:, pq * 1024:(pq + 1) * 1024],
                                           psF[:, 2048:3072], rR[:, qs:qs + 1],
                                           xo[:, pq * 1024:(pq + 1) * 1024],
                                           ALU.mult, ALU.add).then_inc(vXB, 1)
                    v.wait_ge(vXB, gq + 1)
                    v.tensor_add(x1t[:, pq * 1024:(pq + 1) * 1024],
                                 x1t[:, pq * 1024:(pq + 1) * 1024], bob[:, :]
                                 ).then_inc(vX1, 1)
                    v.wait_ge(vX1, gq + 1)
                    if gq > 0:
                        v.wait_ge(vLb, gq)
                    v.bn_stats(bnb2[:, 0:6],
                               x1t[:, pq * 1024: pq * 1024 + 512])
                    v.bn_stats(bnb2[:, 6:12],
                               x1t[:, pq * 1024 + 512:(pq + 1) * 1024]).then_inc(vLa, 1)
                    v.wait_ge(vLa, gq + 1)
                    v.bn_aggr(mv2[:, pq * 2:(pq + 1) * 2],
                              bnb2[:, 0:12]).then_inc(vLb, 1)
                    v.wait_ge(aLc, gq + 1)
                    v.scalar_tensor_tensor(ln2s[:, pq * 4 + 2: pq * 4 + 3],
                                           mv2[:, pq * 2: pq * 2 + 1], -1.0,
                                           ln2s[:, pq * 4 + 1: pq * 4 + 2],
                                           ALU.mult, ALU.mult).then_inc(vLc, 1)
                    if qs >= 1:
                        ystev(gq - 1, bank=(0 if qs <= 2 else 4))
            ystev(NQB * 4 - 1, bank=4)
            v.nop().then_inc(bar, 1)

        @blk.sync
        def _(sy):
            sy.wait_ge(bar, BAR1)

            def kchload(qb, c, skip_first=False):
                for hh in range((1 if skip_first else 0), 2):
                    gh = (qb * NCH + c) * 2 + hh
                    if 2 * gh - 4 > 0:
                        sy.wait_ge(tS, 2 * gh - 4)
                    sy.dma_start(out=kch[:, (gh % 3) * 2048:(gh % 3 + 1) * 2048]
                                 .rearrange("q (dt n) -> q dt n", dt=DT),
                                 in_=kscr[c][:, :, hh * 256:(hh + 1) * 256]
                                 .rearrange("dt p n -> p dt n")
                                 ).then_inc(sKRs[gh % 3], 16)

            for qb in range(NQB):
                if qb == 0:
                    kchload(0, 1, skip_first=True)
                    for c in range(2, NCH):
                        kchload(qb, c)
                else:
                    for c in range(1, NCH):
                        kchload(qb, c)
                for kt in range(KT):
                    gv = qb * KT + kt
                    if gv % 2 != 0:
                        continue
                    if gv - 4 > 0:
                        sy.wait_ge(tPV, gv - 4)
                    sl = (gv // 2) % 3
                    sy.dma_start(out=vch[:, sl * 1024:(sl + 1) * 1024],
                                 in_=vscr[kt]).then_inc(sVRs[sl], 16)
                for qs in range(4):
                    gq = qb * 4 + qs
                    if gq - 1 > 0:
                        sy.wait_ge(vXB, gq - 1)
                    sy.dma_start(out=xo[:, (qs % 2) * 1024:(qs % 2 + 1) * 1024],
                                 in_=xin_own[gq]).then_inc(sXOs[gq % 2], 16)
                if qb + 1 < NQB:
                    kchload(qb + 1, 0)
                for qs in range(4):
                    gq = qb * 4 + qs
                    sy.wait_ge(vX1, gq + 1)
                    sy.dma_start(out=x1scr[gq],
                                 in_=x1t[:, (qs % 2) * 1024:(qs % 2 + 1) * 1024]
                                 ).then_inc(sX1Ws[gq % 2], 16)
            for sl in range(2):
                sy.wait_ge(sX1Ws[sl], 16 * (4 * NQB // 2))
            sy.nop().then_inc(bar, 1)

        @blk.tensor
        def _(te):
            def ytr(gq, bank):
                pq = gq % 2
                te.wait_ge(aB4, gq + 1)
                if gq > 0:
                    te.wait_ge(vY, gq)
                for dt in range(DT):
                    ins = te.transpose(
                        psF[:, bank * 512 + dt * 128: bank * 512 + (dt + 1) * 128],
                        yt[:, dt * 128:(dt + 1) * 128],
                        identf[:, :])
                ins.then_inc(tY, 1)

            te.wait_ge(bar, BAR1)
            te.wait_ge(gWO, 16 * DT)
            te.wait_ge(vK2, NCH)
            for dt in range(DT):
                for h2 in range(2):
                    ins = te.matmul(fb(4 + h2, pn=1), bvT[:, dt:dt + 1],
                                    wo_b[:, dt * 1024 + h2 * 512: dt * 1024 + (h2 + 1) * 512],
                                    start=(dt == 0), stop=(dt == DT - 1))
            ins.then_inc(tBO, 1)
            te.wait_ge(vBO, 1)
            for pc in range(Q2P):
                if pc >= 1:
                    te.wait_ge(vNB, pc * Q2G)
                for dt in range(DT):
                    k = pc * DT + dt
                    te.wait_ge(aNB, k + 1)
                    ins = te.matmul(fb(4, pn=1), onescol[:, :],
                                    qsqb[:, (k % 2) * Q2L:(k % 2 + 1) * Q2L],
                                    start=(dt == 0), stop=(dt == DT - 1))
                    ins.then_inc(tNB, 1)
            for j in range(2):
                ins = te.matmul(psF[:, j * 512:(j + 1) * 512], ones1f[:, :],
                                bopr[0:1, j * 512:(j + 1) * 512],
                                start=True, stop=True)
            ins.then_inc(tBO, 1)
            te.wait_ge(vNB, NQ2 + 3)
            te.matmul(fb(4, cols=1), ones1f[:, :], mk2[0:1, 6:7],
                      start=True, stop=True).then_inc(tBO, 1)
            te.wait_ge(vNB, NQ2 + 4)
            te.wait_ge(vBO, 2)

            SROW0 = min(9, KT - 1)
            for qb in range(NQB):
                if qb >= 1:
                    te.wait_ge(aB2, qb * 8)
                if qb >= 1:
                    te.wait_ge(vY, qb * 4 - 2)
                for kt in range(KT):
                    if qb >= 1 and kt in (2, 3):
                        if kt == 2:
                            te.wait_ge(vX1, qb * 4)
                        ytr(qb * 4 - 2 + (kt - 2), bank=4)
                    c = kt // 4
                    gS = qb * KT + kt
                    gh = (qb * NCH + c) * 2 + (kt % 4) // 2
                    if kt % 2 == 0:
                        te.wait_ge(sKRs[gh % 3], 16 * (gh // 3 + 1))
                    if gS - 3 > 0:
                        te.wait_ge(aB1, gS - 3)
                    for dt in range(DT):
                        ins = te.matmul(fb(gS % 4),
                                        kch[:, (gh % 3) * 2048 + dt * 256 + (kt % 2) * 128:
                                            (gh % 3) * 2048 + dt * 256 + (kt % 2 + 1) * 128],
                                        qT[:, dt * T + qb * 512: dt * T + (qb + 1) * 512],
                                        start=(dt == 0), stop=(dt == DT - 1))
                    ins.then_inc(tS, 1)
                    if kt >= SROW0:
                        if kt == SROW0 and qb >= 1:
                            te.wait_ge(vX1, qb * 4)
                            te.wait_ge(vY, qb * 4)
                        te.wait_ge(aB1, qb * KT + kt)
                        te.matmul(fb(4, pn=1), onescol[:, :],
                                  pT[:, (kt - 1) * 512: kt * 512],
                                  start=(kt == SROW0), stop=False)
                te.wait_ge(aB1, (qb + 1) * KT)
                te.matmul(fb(4, pn=1), onescol[:, :],
                          pT[:, (KT - 1) * 512: KT * 512],
                          start=False, stop=False)
                for j in range(SROW0 - 1):
                    ins = te.matmul(fb(4, pn=1), onescol[:, :],
                                    pT[:, j * 512:(j + 1) * 512],
                                    start=False, stop=(j == SROW0 - 2))
                ins.then_inc(tSs, 1)
                te.wait_ge(vS, 3 * qb + 1)
                for qs in range(4):
                    ins = te.matmul(fb(5, cols=1, off=qs),
                                    srow[0:1, qs * 128:(qs + 1) * 128],
                                    ones1f[0:1, 0:1], start=True, stop=True)
                ins.then_inc(tSr, 1)
                te.wait_ge(vS, 3 * qb + 2)
                for kt in range(KT):
                    gv = qb * KT + kt
                    if gv % 2 == 0:
                        sl = (gv // 2) % 3
                        te.wait_ge(sVRs[sl], 16 * (gv // 6 + 1))
                    else:
                        sl = 3 + (gv // 2) % 2
                        te.wait_ge(gVRs[sl - 3], 16 * (gv // 4 + 1))
                    for ct in range(8):
                        ob = fb(ct) if ct < 6 else psF2[:, (ct - 6) * 512:(ct - 5) * 512]
                        ins = te.matmul(
                            ob,
                            vch[:, sl * 1024 + ct * 128: sl * 1024 + (ct + 1) * 128],
                            pT[:, kt * 512:(kt + 1) * 512],
                            start=(kt == 0), stop=(kt == KT - 1))
                    ins.then_inc(tPV, 1)
                te.wait_ge(aB2, qb * 8 + 8)
                te.wait_ge(vS, 3 * qb + 2)
                for qs in range(4):
                    if qs == 0:
                        if qb >= 1:
                            te.wait_ge(vX1, qb * 4)
                    else:
                        te.wait_ge(vX1, qb * 4 + qs)
                    for ch in range(2):
                        for ct in range(DT):
                            ins = te.matmul(
                                fb(4 + ch),
                                aT[:, ct * 512 + qs * 128: ct * 512 + (qs + 1) * 128],
                                wo_b[:, ct * 1024 + ch * 512: ct * 1024 + (ch + 1) * 512],
                                start=(ct == 0), stop=(ct == DT - 1))
                        ins.then_inc(tO, 1)
                    if 1 <= qs <= 2:
                        ytr(qb * 4 + qs - 1, bank=0)
                if qb == NQB - 1:
                    te.wait_ge(vX1, NQB * 4)
                    ytr(qb * 4 + 2, bank=4)
                    ytr(qb * 4 + 3, bank=4)
            te.nop().then_inc(bar, 1)

    bscope.close()
    ab.close()

    cscope = contextlib.ExitStack()
    sbC = lambda name, shape, dt: cscope.enter_context(nc.sbuf_tensor(name, shape, dt))
    acc = sbC("acc", [128, NQT * 1024], F32)
    zb = sbC("zb", [128, HC * T], BF16)
    HH = H // NSPE
    w1b = sbC("w1b", [128, 2 * DT * HH], BF16)
    w2b = sbC("w2b", [128, 2 * DT * HH], BF16)
    w3b = sbC("w3b", [128, 2 * HC * 1024], BF16)
    silu_st = sbC("silu_st", [128, 2 * 512], BF16)
    wgb = sbC("wgb", [128, DT * E], BF16)
    comb = sbC("comb", [128, NQT * E], F32)
    gm1 = sbC("gm1", [128, NQT], F32)
    gm2 = sbC("gm2", [128, NQT], F32)
    gd = sbC("gd", [128, NQT], F32)
    gw1 = sbC("gw1", [128, NQT], F32)
    gw2 = sbC("gw2", [128, NQT], F32)
    geq = sbC("geq", [128, NQT * E], F32)
    geq2 = sbC("geq2", [128, NQT * E], F32)
    gmsk = sbC("gmsk", [128, NQT * E], F32)


    with nc.Block("phaseC") as blk:

        @blk.gpsimd
        def _(g):
            g.wait_ge(bar, BAR2)
            g.dma_start(out=wgb[:, :], in_=Wg).then_inc(gWG, 16)
            for dt in range(DT):
                g.dma_start(out=w1b[:, dt * HH:(dt + 1) * HH],
                            in_=w1v[0, dt][:, 0:HH]).then_inc(gW1s[0], 16)
            for dt in range(DT):
                g.dma_start(out=w2b[:, dt * HH:(dt + 1) * HH],
                            in_=w2v[0, dt][:, 0:HH]).then_inc(gW2s[0], 16)
            for e in range(E):
                for sh in range(NSPE):
                    sub = NSPE * e + sh
                    if sub >= 1:
                        if sub >= 2:
                            g.wait_ge(tH, (sub - 1) * UH1)
                        for dt in range(DT):
                            g.dma_start(out=w1b[:, (sub % 2) * DT * HH + dt * HH:
                                               (sub % 2) * DT * HH + (dt + 1) * HH],
                                        in_=w1v[e, dt][:, sh * HH:(sh + 1) * HH]
                                        ).then_inc(gW1s[sub % 2], 16)
                        for dt in range(DT):
                            g.dma_start(out=w2b[:, (sub % 2) * DT * HH + dt * HH:
                                               (sub % 2) * DT * HH + (dt + 1) * HH],
                                        in_=w2v[e, dt][:, sh * HH:(sh + 1) * HH]
                                        ).then_inc(gW2s[sub % 2], 16)
                    if sub >= 2:
                        g.wait_ge(tO2, (sub - 1) * UO2)
                    for ht in range(HC):
                        g.dma_start(out=w3b[:, (sub % 2) * HC * 1024 + ht * 1024:
                                           (sub % 2) * HC * 1024 + (ht + 1) * 1024],
                                    in_=w3v[e, sh * HC + ht]).then_inc(gW3s[sub % 2], 16)
            g.wait_ge(gWG, 16)
            for sl in range(2):
                g.wait_ge(gW1s[sl], 16 * DT * (NSUB // 2))
                g.wait_ge(gW2s[sl], 16 * DT * (NSUB // 2))
                g.wait_ge(gW3s[sl], 16 * HC * (NSUB // 2))
            g.nop().then_inc(bar, 1)

        @blk.sync
        def _(sy):
            sy.wait_ge(bar, BAR2)
            npiece = max(1, NQT // 4)
            for j in range(NQT // npiece):
                sy.dma_start(out=acc[:, j * npiece * 1024:(j + 1) * npiece * 1024]
                             .rearrange("p (t n) -> p t n", n=1024),
                             in_=x1scr[j * npiece:(j + 1) * npiece]
                             .rearrange("t p n -> p t n")).then_inc(sACC, 16)
            for tt in range(NQT):
                sy.wait_ge(vAcc, (NSUB - 1) * UO2 + tt * 2 + 2)
                sy.dma_start(out=outv[tt], in_=acc[:, tt * 1024:(tt + 1) * 1024]
                             ).then_inc(sOUT, 16)
            sy.wait_ge(sOUT, 16 * NQT)

        @blk.tensor
        def _(te):
            te.wait_ge(bar, BAR2)
            te.wait_ge(gWG, 16)
            for tt in range(NQT):
                for dt in range(DT):
                    ins = te.matmul(fb(4, cols=E, off=E * tt),
                                    yTb[:, dt * T + tt * 128: dt * T + (tt + 1) * 128],
                                    wgb[:, dt * E:(dt + 1) * E],
                                    start=(dt == 0), stop=(dt == DT - 1))
                ins.then_inc(tG, 1)
            for e in range(E):
                for sh in range(NSPE):
                    sub = NSPE * e + sh
                    te.wait_ge(gW1s[sub % 2], 16 * DT * (sub // 2 + 1))
                    te.wait_ge(gW2s[sub % 2], 16 * DT * (sub // 2 + 1))
                    if sub >= 1:
                        last3 = UO2 - 1 - ((UO2 - 3) % 3)
                        te.wait_ge(vAcc, (sub - 1) * UO2 + last3 + 1)
                    for u in range(UH1):
                        tq, ht = divmod(u, HC)
                        gu = sub * UH1 + u
                        if gu - 2 >= 0:
                            te.wait_ge(aSl, gu - 1)
                            te.wait_ge(vZ, gu - 1)
                        for dt in range(DT):
                            te.matmul(
                                fb(2 * (u % 2)),
                                w1b[:, (sub % 2) * DT * HH + dt * HH + ht * 128:
                                    (sub % 2) * DT * HH + dt * HH + (ht + 1) * 128],
                                yTb[:, dt * T + tq * 512: dt * T + (tq + 1) * 512],
                                start=(dt == 0), stop=(dt == DT - 1))
                        for dt in range(DT):
                            ins = te.matmul(
                                fb(2 * (u % 2) + 1),
                                w2b[:, (sub % 2) * DT * HH + dt * HH + ht * 128:
                                    (sub % 2) * DT * HH + dt * HH + (ht + 1) * 128],
                                yTb[:, dt * T + tq * 512: dt * T + (tq + 1) * 512],
                                start=(dt == 0), stop=(dt == DT - 1))
                        ins.then_inc(tH, 1)
                    te.wait_ge(gW3s[sub % 2], 16 * HC * (sub // 2 + 1))
                    if sub == 0:
                        te.wait_ge(vG, 9)
                    for u2 in range(UO2):
                        tt, ch = divmod(u2, 2)
                        gu2 = sub * UO2 + u2
                        if ch == 0 and tt % 4 == 0:
                            te.wait_ge(vZ, sub * UH1 + (tt // 4 + 1) * HC)
                        if u2 == 2:
                            te.wait_ge(vZ, (sub + 1) * UH1)
                        if gu2 - 3 >= 0:
                            te.wait_ge(vAcc, gu2 - 2)
                        for ht in range(HC):
                            ins = te.matmul(
                                OBANK(u2),
                                zb[:, ht * T + tt * 128: ht * T + (tt + 1) * 128],
                                w3b[:, (sub % 2) * HC * 1024 + ht * 1024 + ch * 512:
                                    (sub % 2) * HC * 1024 + ht * 1024 + (ch + 1) * 512],
                                start=(ht == 0), stop=(ht == HC - 1))
                        ins.then_inc(tO2, 1)
            te.nop().then_inc(bar, 1)

        @blk.scalar
        def _(s):
            s.wait_ge(bar, BAR2)
            s.wait_ge(vG, 5)
            s.activation(gd[:, :], gd[:, :], AF.Exp, scale=-1.0).then_inc(aG, 1)
            for sub in range(NSUB):
                for u in range(UH1):
                    gu = sub * UH1 + u
                    s.wait_ge(tH, gu + 1)
                    if gu - 2 >= 0:
                        s.wait_ge(vZ, gu - 1)
                    s.activation(silu_st[:, (u % 2) * 512:(u % 2 + 1) * 512],
                                 fb(2 * (u % 2)), AF.Silu).then_inc(aSl, 1)
            s.nop().then_inc(bar, 1)

        @blk.vector
        def _(v):
            v.wait_ge(bar, BAR2)
            v.wait_ge(tG, NQT)
            for tt in range(NQT):
                ins = v.reduce_max(gm1[:, tt:tt + 1], fb(4, cols=E, off=E * tt),
                                   axis=AX.X)
            ins.then_inc(vG, 1)
            v.wait_ge(vG, 1)
            for tt in range(NQT):
                ins = v.tensor_single_scalar(geq[:, E * tt:E * (tt + 1)],
                                             fb(4, cols=E, off=E * tt),
                                             gm1[:, tt:tt + 1], ALU.is_equal)
            ins.then_inc(vGb, 1)
            v.wait_ge(vGb, 1)
            for tt in range(NQT):
                ins = v.scalar_tensor_tensor(gmsk[:, E * tt:E * (tt + 1)],
                                             geq[:, E * tt:E * (tt + 1)], -1e30,
                                             fb(4, cols=E, off=E * tt),
                                             ALU.mult, ALU.add)
            ins.then_inc(vG, 1)
            v.wait_ge(vG, 2)
            for tt in range(NQT):
                ins = v.reduce_max(gm2[:, tt:tt + 1], gmsk[:, E * tt:E * (tt + 1)],
                                   axis=AX.X)
            ins.then_inc(vG, 1)
            v.wait_ge(vG, 3)
            for tt in range(NQT):
                ins = v.tensor_single_scalar(geq2[:, E * tt:E * (tt + 1)],
                                             gmsk[:, E * tt:E * (tt + 1)],
                                             gm2[:, tt:tt + 1], ALU.is_equal)
            ins.then_inc(vG, 1)
            v.wait_ge(vG, 4)
            v.tensor_sub(gd[:, :], gm1[:, :], gm2[:, :]).then_inc(vG, 1)
            v.wait_ge(aG, 1)
            v.tensor_scalar(gw1[:, :], gd[:, :], 1.0, None, ALU.add).then_inc(vG, 1)
            v.wait_ge(vG, 6)
            v.reciprocal(gw1[:, :], gw1[:, :]).then_inc(vG, 1)
            v.wait_ge(vG, 7)
            v.tensor_scalar(gw2[:, :], gw1[:, :], -1.0, 1.0, ALU.mult, ALU.add
                            ).then_inc(vG, 1)
            v.wait_ge(vG, 8)
            for tt in range(NQT):
                ins = v.tensor_scalar(comb[:, E * tt:E * (tt + 1)],
                                      geq[:, E * tt:E * (tt + 1)],
                                      gw1[:, tt:tt + 1], None, ALU.mult)
            ins.then_inc(vGb, 1)
            v.wait_ge(vGb, 2)
            for tt in range(NQT):
                ins = v.scalar_tensor_tensor(comb[:, E * tt:E * (tt + 1)],
                                             geq2[:, E * tt:E * (tt + 1)],
                                             gw2[:, tt:tt + 1],
                                             comb[:, E * tt:E * (tt + 1)],
                                             ALU.mult, ALU.add)
            ins.then_inc(vG, 1)
            npiece = max(1, NQT // 4)
            v.wait_ge(sACC, 16 * (NQT // npiece))
            for sub in range(NSUB):
                e = sub // NSPE
                for u in range(UH1):
                    tq, ht = divmod(u, HC)
                    gu = sub * UH1 + u
                    v.wait_ge(aSl, gu + 1)
                    if sub >= 1 and u == 0:
                        v.wait_ge(tO2, sub * UO2)
                    v.tensor_tensor(zb[:, ht * T + tq * 512: ht * T + (tq + 1) * 512],
                                    silu_st[:, (u % 2) * 512:(u % 2 + 1) * 512],
                                    fb(2 * (u % 2) + 1), ALU.mult).then_inc(vZ, 1)
                for u2 in range(UO2):
                    tt, ch = divmod(u2, 2)
                    gu2 = sub * UO2 + u2
                    v.wait_ge(tO2, gu2 + 1)
                    v.scalar_tensor_tensor(
                        acc[:, tt * 1024 + ch * 512: tt * 1024 + (ch + 1) * 512],
                        OBANK(u2),
                        comb[:, tt * E + e: tt * E + e + 1],
                        acc[:, tt * 1024 + ch * 512: tt * 1024 + (ch + 1) * 512],
                        ALU.mult, ALU.add).then_inc(vAcc, 1)
            v.nop().then_inc(bar, 1)

    cscope.close()
    ps2.close()
    stack.close()
    return nc


_CACHE = {}


def _get_nc():
    if "full" not in _CACHE:
        _CACHE["full"] = build(Cfg())
    return _CACHE["full"]


def _shard(x, Wq, bq_, Wk, bk_, Wv, bv_, Wo, bo_, Wg_, W1, W2, W3, e):
    B, TT, _ = x.shape
    TO = TT // 4

    def bias_layout(b):
        return np.ascontiguousarray(np.asarray(b, np.float32).reshape(DT, 128).T)

    wg = np.asarray(Wg_, np.float32)
    wg_dev = np.ascontiguousarray(
        wg.reshape(DT, 128, e).transpose(1, 0, 2).reshape(128, DT * e))
    common = {
        "Wq": np.ascontiguousarray(np.asarray(Wq, np.float32)),
        "Wk": np.ascontiguousarray(np.asarray(Wk, np.float32)),
        "Wv": np.ascontiguousarray(np.asarray(Wv, np.float32)),
        "Wo": np.ascontiguousarray(np.asarray(Wo, np.float32)),
        "bq": bias_layout(bq_), "bk": bias_layout(bk_), "bv": bias_layout(bv_),
        "bo": np.ascontiguousarray(np.asarray(bo_, np.float32).reshape(1, D)),
        "Wg": wg_dev,
        "W1": np.ascontiguousarray(np.asarray(W1, np.float32)),
        "W2": np.ascontiguousarray(np.asarray(W2, np.float32)),
        "W3": np.ascontiguousarray(np.asarray(W3, np.float32)),
    }
    in_maps = []
    for core in range(NCORES):
        b, q = core // 4, core % 4
        m = dict(common)
        m["x_own"] = np.ascontiguousarray(x[b, q * TO:(q + 1) * TO])
        m["x_rest"] = np.ascontiguousarray(
            np.concatenate([x[b, :q * TO], x[b, (q + 1) * TO:]], axis=0))
        in_maps.append(m)
    return in_maps


def kernel(L0_attn_x, L0_attn_Wq, L0_attn_bq, L0_attn_Wk, L0_attn_bk,
           L0_attn_Wv, L0_attn_bv, L0_attn_Wo, L0_attn_bo,
           L0_ffn_Wg, L0_ffn_W1, L0_ffn_W2, L0_ffn_W3):
    x = np.ascontiguousarray(np.asarray(L0_attn_x, dtype=np.float32))
    B, TT, _ = x.shape
    TO = TT // 4
    in_maps = _shard(x, L0_attn_Wq, L0_attn_bq, L0_attn_Wk, L0_attn_bk,
                     L0_attn_Wv, L0_attn_bv, L0_attn_Wo, L0_attn_bo,
                     L0_ffn_Wg, L0_ffn_W1, L0_ffn_W2, L0_ffn_W3, 8)
    nc = _get_nc()
    res = run_bass_kernel_spmd(nc, in_maps, core_ids=list(range(NCORES)))
    outp = np.empty((B, TT, D), np.float32)
    for core in range(NCORES):
        b, q = core // 4, core % 4
        outp[b, q * TO:(q + 1) * TO] = res.results[core]["out"]
    return outp


# version 26
# speedup vs baseline: 1.1999x; 1.0025x over previous
import contextlib
import numpy as np

import concourse.bass as bass
import concourse.mybir as mybir
from concourse.bass_utils import run_bass_kernel_spmd

F32 = mybir.dt.float32
BF16 = mybir.dt.bfloat16
AF = mybir.ActivationFunctionType
ALU = mybir.AluOpType
AX = mybir.AxisListType

D = 1024
DT = 8
NCORES = 8
EPS = 1e-5
LOG_INV32 = float(np.log(1.0 / 32.0))


class Cfg:
    def __init__(self, t_b=8192, t_own=2048, h=2048, e=8):
        assert t_b % 512 == 0 and t_own % 512 == 0 and h % 256 == 0
        self.t_b, self.t_own, self.h, self.e = t_b, t_own, h, e
        self.NCH = t_b // 512
        self.KT = t_b // 128
        self.NQT = t_own // 128
        self.NQB = t_own // 512
        self.OWN_CH = t_own // 512
        self.NSPE = 4 if h >= 1024 else 2
        self.HC = (h // 128) // self.NSPE
        self.NTQ = t_own // 512
        self.UH1 = self.HC * self.NTQ
        self.UO2 = self.NQT * 2
        self.NSUB = e * self.NSPE
        self.Q2P = t_own // 512
        self.Q2L = 512
        self.Q2G = 1


def build(cfg: Cfg):
    nc = bass.Bass("TRN2", target_bir_lowering=False)
    C = cfg
    T, TB, H, E = C.t_own, C.t_b, C.h, C.e
    NCH, KT, NQT, NQB, OWN_CH = C.NCH, C.KT, C.NQT, C.NQB, C.OWN_CH
    HC, NTQ, UH1, UO2, NSUB, NSPE = C.HC, C.NTQ, C.UH1, C.UO2, C.NSUB, C.NSPE

    x_own = nc.dram_tensor("x_own", [T, D], F32, kind="ExternalInput").ap()
    x_rest = nc.dram_tensor("x_rest", [TB - T, D], F32, kind="ExternalInput").ap()
    Wq = nc.dram_tensor("Wq", [D, D], F32, kind="ExternalInput").ap()
    Wk = nc.dram_tensor("Wk", [D, D], F32, kind="ExternalInput").ap()
    Wv = nc.dram_tensor("Wv", [D, D], F32, kind="ExternalInput").ap()
    Wo = nc.dram_tensor("Wo", [D, D], F32, kind="ExternalInput").ap()
    bq = nc.dram_tensor("bq", [128, DT], F32, kind="ExternalInput").ap()
    bk = nc.dram_tensor("bk", [128, DT], F32, kind="ExternalInput").ap()
    bv = nc.dram_tensor("bv", [128, DT], F32, kind="ExternalInput").ap()
    bo = nc.dram_tensor("bo", [1, D], F32, kind="ExternalInput").ap()
    Wg = nc.dram_tensor("Wg", [128, DT * E], F32, kind="ExternalInput").ap()
    W1 = nc.dram_tensor("W1", [E, D, H], F32, kind="ExternalInput").ap()
    W2 = nc.dram_tensor("W2", [E, D, H], F32, kind="ExternalInput").ap()
    W3 = nc.dram_tensor("W3", [E, H, D], F32, kind="ExternalInput").ap()
    out = nc.dram_tensor("out", [T, D], F32, kind="ExternalOutput").ap()

    kscr = nc.dram_tensor("kscr", [NCH, DT, 128, 512], BF16, kind="Internal").ap()
    vscr = nc.dram_tensor("vscr", [KT, 128, D], BF16, kind="Internal").ap()
    x1scr = nc.dram_tensor("x1scr", [NQT, 128, D], F32, kind="Internal").ap()

    xin_own = x_own.rearrange("(t p) d -> t p d", p=128)
    xin_rest = x_rest.rearrange("(t p) d -> t p d", p=128)
    outv = out.rearrange("(t p) d -> t p d", p=128)

    def xtile(t):
        return xin_own[t] if t < NQT else xin_rest[t - NQT]

    wview = {nm: w.rearrange("(dt p) d -> dt p d", p=128)
             for nm, w in (("q", Wq), ("k", Wk), ("v", Wv), ("o", Wo))}
    w1v = W1.rearrange("e (dt p) h -> e dt p h", p=128)
    w2v = W2.rearrange("e (dt p) h -> e dt p h", p=128)
    w3v = W3.rearrange("e (ht p) d -> e ht p d", p=128)

    stack = contextlib.ExitStack()
    sb = lambda name, shape, dt: stack.enter_context(nc.sbuf_tensor(name, shape, dt))
    sem = lambda name: stack.enter_context(nc.semaphore(name))

    psF = stack.enter_context(nc.psum_tensor("psF", [128, 3072], F32))

    def fb(bank, cols=512, off=0, p0=0, pn=128):
        return psF[p0:p0 + pn, bank * 512 + off: bank * 512 + off + cols]

    def fbank6(par):
        return (psB6 if par == 0 else psB7)[:, :]

    sXs = [sem(f"sX{i}") for i in range(8)]
    sKWs = [sem(f"sKW{i}") for i in range(4)]
    gVWs = [sem(f"gVW{i}") for i in range(4)]
    sKRs = [sem(f"sKR{i}") for i in range(3)]
    sVRs = [sem(f"sVR{i}") for i in range(3)]
    gVRs = [sem(f"gVR{i}") for i in range(2)]
    sXOs = [sem(f"sXO{i}") for i in range(2)]
    sX1Ws = [sem(f"sX1W{i}") for i in range(2)]
    sACC, sOUT = sem("sACC"), sem("sOUT")
    gBI, gWK, gWV, gWQ, gWO, gWG = (sem("gBI"), sem("gWK"), sem("gWV"),
                                    sem("gWQ"), sem("gWO"), sem("gWG"))
    gW1s = [sem(f"gW1{i}") for i in range(2)]
    gW2s = [sem(f"gW2{i}") for i in range(2)]
    gW3s = [sem(f"gW3{i}") for i in range(2)]
    gI = sem("gI")
    tTR, tK, tV, tQ, tK2 = sem("tTR"), sem("tK"), sem("tV"), sem("tQ"), sem("tK2")
    tK2m = sem("tK2m")
    tNB, tBO, tSr = sem("tNB"), sem("tBO"), sem("tSr")
    tS, tSs, tPV, tO, tY = sem("tS"), sem("tSs"), sem("tPV"), sem("tO"), sem("tY")
    tG, tH, tO2 = sem("tG"), sem("tH"), sem("tO2")
    vA1, vA2, vA3, vA4, vK2 = sem("vA1"), sem("vA2"), sem("vA3"), sem("vA4"), sem("vK2")
    vBQ, vNB, vBO = sem("vBQ"), sem("vNB"), sem("vBO")
    vS, vXB, vX1 = sem("vS"), sem("vXB"), sem("vX1")
    vLa, vLb, vLc, vY = sem("vLa"), sem("vLb"), sem("vLc"), sem("vY")
    vG, vZ, vAcc = sem("vG"), sem("vZ"), sem("vAcc")
    vGb = sem("vGb")
    aA1, aA2, vAp = sem("aA1"), sem("aA2"), sem("vAp")
    aK, aV, aQ = sem("aK"), sem("aV"), sem("aQ")
    aNB, aNB2, aXT = sem("aNB"), sem("aNB2"), sem("aXT")
    aB1, aB2, aLa, aLc, aB4 = sem("aB1"), sem("aB2"), sem("aLa"), sem("aLc"), sem("aB4")
    aG, aSl = sem("aG"), sem("aSl")
    bar = sem("bar")
    BAR1, BAR2 = 5, 10

    ident = sb("ident", [128, 128], BF16)
    identf = sb("identf", [128, 128], F32)
    ones1b = sb("ones1b", [1, 128], BF16)
    ones1f = sb("ones1f", [1, 128], F32)
    onescol = sb("onescol", [128, 1], BF16)
    bqs = sb("bqs", [128, DT], F32)
    bqc = sb("bqc", [128, DT], F32)
    bkc = sb("bkc", [128, DT], F32)
    bvT = sb("bvT", [128, DT], BF16)
    k2m = sb("k2m", [1, 16], F32)
    yTb = stack.enter_context(nc.sbuf_tensor("yTb", [128, DT * T], BF16, side="right"))
    epsc = sb("epsc", [128, 1], F32)
    kch = sb("kch", [128, 3 * 2048], BF16)

    GW_B = 64
    GW_K = GW_B + DT * 16
    GW_V = GW_B + 2 * DT * 16
    GW_Q = GW_B + 3 * DT * 16
    GW_O = GW_B + 4 * DT * 16

    ab = contextlib.ExitStack()
    sbAB = lambda name, shape, dt: ab.enter_context(nc.sbuf_tensor(name, shape, dt))
    qT = sbAB("qT", [128, DT * T], BF16)
    wo_b = sbAB("wo_b", [128, DT * 1024], BF16)

    aonly = contextlib.ExitStack()
    sbA = lambda name, shape, dt: aonly.enter_context(nc.sbuf_tensor(name, shape, dt))
    psB6 = aonly.enter_context(nc.psum_tensor("psB6", [128, 1024], BF16))
    psB7 = aonly.enter_context(nc.psum_tensor("psB7", [128, 1024], BF16))
    wq_b = sbA("wq_b", [128, DT * 1024], BF16)
    wk_b = sbA("wk_b", [128, DT * 1024], BF16)
    wv_b = sbA("wv_b", [128, DT * 1024], BF16)
    xb = sbA("xb", [128, 6 * 1024], F32)
    xnb = sbA("xnb", [128, 4 * 1024], BF16)
    xnT = sbA("xnT", [128, 2 * 4096], BF16)
    kst = sbA("kst", [128, 4 * 512], BF16)
    vst = sbA("vst", [128, 4 * 512], BF16)
    ksq = sbA("ksq", [128, 8 * 512], BF16)
    bnb = sbA("bnb", [128, 2 * 48], F32)
    mv = sbA("mv", [128, 2 * 8], F32)
    lnt = sbA("lnt", [128, 2 * 4], F32)
    rsb = sbA("rsb", [128, 2 * 4], F32)
    nmrs = sbA("nmrs", [128, 2 * 4], F32)
    wbuf = {"q": wq_b, "k": wk_b, "v": wv_b, "o": wo_b}

    with nc.Block("phaseA") as blk:

        @blk.gpsimd
        def _(g):
            g.memset(ident[:, :], 0.0)
            g.memset(identf[:, :], 0.0).then_inc(gI, 1)
            g.wait_ge(gI, 1)
            g.affine_select(out=ident[:, :], in_=ident[:, :],
                            compare_op=ALU.not_equal, fill=1.0, base=0,
                            pattern=[[-1, 128]], channel_multiplier=1)
            g.affine_select(out=identf[:, :], in_=identf[:, :],
                            compare_op=ALU.not_equal, fill=1.0, base=0,
                            pattern=[[-1, 128]], channel_multiplier=1).then_inc(gI, 1)
            g.memset(ones1b[:, :], 1.0)
            g.memset(ones1f[:, :], 1.0)
            g.memset(onescol[:, :], 1.0)
            g.memset(epsc[:, :], EPS).then_inc(gI, 1)
            g.dma_start(out=bqc[:, :], in_=bq).then_inc(gBI, 16)
            g.dma_start(out=bkc[:, :], in_=bk).then_inc(gBI, 16)
            g.dma_start(out=bvT[:, :], in_=bv).then_inc(gBI, 16)
            wsem = {"k": gWK, "v": gWV, "q": gWQ, "o": gWO}
            for nm in ("k", "v", "q", "o"):
                for dt in range(DT):
                    g.dma_start(out=wbuf[nm][:, dt * 1024:(dt + 1) * 1024],
                                in_=wview[nm][dt]).then_inc(wsem[nm], 16)
            for c in range(NCH):
                for u in range(8):
                    g.wait_ge(aV, 8 * c + u + 1)
                    kt, half = 4 * c + u // 2, u % 2
                    g.dma_start(out=vscr[kt][:, half * 512:(half + 1) * 512],
                                in_=vst[:, (u % 4) * 512:(u % 4 + 1) * 512]
                                ).then_inc(gVWs[u % 4], 16)
            g.wait_ge(gBI, 48)
            for s_ in (gWK, gWV, gWQ, gWO):
                g.wait_ge(s_, 16 * DT)
            for sl in range(4):
                g.wait_ge(gVWs[sl], 16 * 2 * NCH)
            g.nop().then_inc(bar, 1)

        @blk.sync
        def _(sy):
            for c in range(-2, NCH):
                if c + 2 < NCH:
                    for i in range(4):
                        t = 4 * (c + 2) + i
                        if t >= 6:
                            sy.wait_ge(vAp, t - 5)
                        sy.dma_start(out=xb[:, (t % 6) * 1024:(t % 6 + 1) * 1024],
                                     in_=xtile(t)).then_inc(sXs[t % 6], 16)
                if c >= 0:
                    for ot in range(DT):
                        sy.wait_ge(aK, 8 * c + ot + 1)
                        sy.dma_start(out=kscr[c, ot],
                                     in_=kst[:, (ot % 4) * 512:(ot % 4 + 1) * 512]
                                     ).then_inc(sKWs[ot % 4], 16)
            for sl in range(4):
                sy.wait_ge(sKWs[sl], 16 * 2)
            for hh in range(2):
                sy.dma_start(out=kch[:, hh * 2048:(hh + 1) * 2048]
                             .rearrange("q (dt n) -> q dt n", dt=DT),
                             in_=kscr[0][:, :, hh * 256:(hh + 1) * 256]
                             .rearrange("dt p n -> p dt n")
                             ).then_inc(sKRs[hh], 16)
            sy.wait_ge(sKWs[0], 16 * 4)
            sy.wait_ge(sKWs[1], 16 * 4)
            sy.wait_ge(sKWs[2], 16 * 4)
            sy.wait_ge(sKWs[3], 16 * 4)
            sy.dma_start(out=kch[:, 2 * 2048:3 * 2048]
                         .rearrange("q (dt n) -> q dt n", dt=DT),
                         in_=kscr[1][:, :, 0:256]
                         .rearrange("dt p n -> p dt n")
                         ).then_inc(sKRs[2], 16)
            for sl in range(4):
                sy.wait_ge(sKWs[sl], 16 * 2 * NCH)
            for sl in range(6):
                nsl = len([t for t in range(4 * NCH) if t % 6 == sl])
                sy.wait_ge(sXs[sl], 16 * nsl)
            sy.nop().then_inc(bar, 1)

        @blk.vector
        def _(v):
            v.wait_ge(gBI, 48)
            v.tensor_scalar(bqs[:, :], bqc[:, :], 1.0 / 32.0, None, ALU.mult
                            ).then_inc(vBQ, 1)
            def stats_chunk(c):
                p = c % 2
                for i in range(4):
                    t = 4 * c + i
                    v.wait_ge(sXs[t % 6], 16 * (t // 6 + 1))
                if c >= 2:
                    v.wait_ge(vA2, c - 1)
                for i in range(4):
                    t = 4 * c + i
                    v.bn_stats(bnb[:, p * 48 + 12 * i: p * 48 + 12 * i + 6],
                               xb[:, (t % 6) * 1024:(t % 6) * 1024 + 512])
                    ins = v.bn_stats(bnb[:, p * 48 + 12 * i + 6: p * 48 + 12 * (i + 1)],
                                     xb[:, (t % 6) * 1024 + 512:(t % 6 + 1) * 1024])
                ins.then_inc(vA1, 1)
                v.wait_ge(vA1, c + 1)
                if c >= 2:
                    v.wait_ge(vA3, c - 1)
                for i in range(4):
                    ins = v.bn_aggr(mv[:, p * 8 + 2 * i: p * 8 + 2 * i + 2],
                                    bnb[:, p * 48 + 12 * i: p * 48 + 12 * (i + 1)])
                ins.then_inc(vA2, 1)

            stats_chunk(0)
            for c in range(NCH):
                p = c % 2
                v.wait_ge(aA2, c + 1)
                for i in range(4):
                    ins = v.scalar_tensor_tensor(
                        nmrs[:, p * 4 + i: p * 4 + i + 1],
                        mv[:, p * 8 + 2 * i: p * 8 + 2 * i + 1], -1.0,
                        rsb[:, p * 4 + i: p * 4 + i + 1], ALU.mult, ALU.mult)
                ins.then_inc(vA3, 1)
                v.wait_ge(vA3, c + 1)
                for i in range(4):
                    t = 4 * c + i
                    if t - 3 > 0:
                        v.wait_ge(tTR, t - 3)
                    v.tensor_scalar(xnb[:, (t % 4) * 1024:(t % 4 + 1) * 1024],
                                    xb[:, (t % 6) * 1024:(t % 6 + 1) * 1024],
                                    rsb[:, p * 4 + i: p * 4 + i + 1],
                                    nmrs[:, p * 4 + i: p * 4 + i + 1],
                                    ALU.mult, ALU.add).then_inc(vAp, 1)
                if c >= 1:
                    v.wait_ge(tK2m, 8 * c)
                    v.reduce_max(k2m[0:1, c - 1:c], fb(4 + ((c - 1) % 2), pn=1),
                                 axis=AX.X).then_inc(vK2, 1)
                if c + 1 < NCH:
                    stats_chunk(c + 1)
            v.wait_ge(tK2m, 8 * NCH)
            v.reduce_max(k2m[0:1, NCH - 1:NCH], fb(4 + ((NCH - 1) % 2), pn=1),
                         axis=AX.X).then_inc(vK2, 1)
            v.nop().then_inc(bar, 1)

        @blk.scalar
        def _(s):
            def evacsKV(c):
                for ot in range(DT):
                    s.wait_ge(tK, 8 * c + ot + 1)
                    if 8 * c + ot >= 4:
                        s.wait_ge(sKWs[ot % 4], 16 * ((8 * c + ot) // 4))
                    s.activation(kst[:, (ot % 4) * 512:(ot % 4 + 1) * 512],
                                 fb(ot % 2), AF.Identity, bias=bkc[:, ot:ot + 1])
                    if c >= 1:
                        s.wait_ge(tK2m, 8 * (c - 1) + ot + 1)
                    s.activation(ksq[:, ot * 512:(ot + 1) * 512],
                                 fb(ot % 2), AF.Square, bias=bkc[:, ot:ot + 1]
                                 ).then_inc(aK, 1)
                for u in range(8):
                    s.wait_ge(tV, 8 * c + u + 1)
                    if 8 * c + u >= 4:
                        s.wait_ge(gVWs[u % 4], 16 * ((8 * c + u) // 4))
                    s.activation(vst[:, (u % 4) * 512:(u % 4 + 1) * 512],
                                 fb(2 + u % 2), AF.Identity).then_inc(aV, 1)

            def evacsQ(c):
                if c < OWN_CH:
                    s.wait_ge(vBQ, 1)
                    for ot in range(DT):
                        s.wait_ge(tQ, 8 * c + ot + 1)
                        s.activation(qT[:, ot * T + 512 * c: ot * T + 512 * (c + 1)],
                                     fb(ot % 2), AF.Identity, scale=1.0 / 32.0,
                                     bias=bqs[:, ot:ot + 1]).then_inc(aQ, 1)

            s.wait_ge(gI, 3)

            def lnrs(c):
                p = c % 2
                s.wait_ge(vA2, c + 1)
                if c >= 2:
                    s.wait_ge(vAp, 4 * (c - 1))
                for i in range(4):
                    ins = s.activation(lnt[:, p * 4 + i: p * 4 + i + 1],
                                       mv[:, p * 8 + 2 * i + 1: p * 8 + 2 * i + 2],
                                       AF.Ln, bias=epsc[:, 0:1])
                ins.then_inc(aA1, 1)
                s.wait_ge(aA1, c + 1)
                s.activation(rsb[:, p * 4:(p + 1) * 4], lnt[:, p * 4:(p + 1) * 4],
                             AF.Exp, scale=-0.5).then_inc(aA2, 1)

            lnrs(0)
            for c in range(NCH):
                p2 = c % 2
                for i in range(4):
                    t = 4 * c + i
                    s.wait_ge(tTR, t + 1)
                    s.activation(
                        xnT[:, p2 * 4096:(p2 + 1) * 4096]
                        .rearrange("q (dt n) -> q dt n", dt=DT)[:, :, i * 128:(i + 1) * 128],
                        fbank6(t % 2).rearrange("q (dt n) -> q dt n", dt=DT),
                        AF.Identity).then_inc(aXT, 1)
                if c >= 1:
                    evacsKV(c - 1)
                if c + 1 < NCH:
                    lnrs(c + 1)
                if c >= 1:
                    evacsQ(c - 1)
            evacsKV(NCH - 1)
            evacsQ(NCH - 1)
            s.nop().then_inc(bar, 1)

        @blk.tensor
        def _(te):
            def k2mm(c, ot):
                te.wait_ge(aK, 8 * c + ot + 1)
                if ot == 0:
                    te.wait_ge(gI, 3)
                    if c >= 2:
                        te.wait_ge(vK2, c - 1)
                te.matmul(fb(4 + c % 2, pn=1), onescol[:, :],
                          ksq[:, ot * 512:(ot + 1) * 512],
                          start=(ot == 0), stop=(ot == DT - 1)
                          ).then_inc(tK2m, 1)

            def proj(c):
                p = c % 2
                te.wait_ge(aXT, 4 * c + 4)
                if c == 0:
                    te.wait_ge(gWK, 16 * DT)
                if 1 <= c <= OWN_CH:
                    te.wait_ge(aQ, 8 * c)
                for ot in range(DT):
                    if 8 * c + ot - 1 > 0:
                        te.wait_ge(aK, 8 * c + ot - 1)
                    for dt in range(DT):
                        ins = te.matmul(
                            fb(ot % 2),
                            wk_b[:, dt * 1024 + ot * 128: dt * 1024 + (ot + 1) * 128],
                            xnT[:, p * 4096 + dt * 512: p * 4096 + (dt + 1) * 512],
                            start=(dt == 0), stop=(dt == DT - 1))
                    ins.then_inc(tK, 1)
                    if ot >= 1:
                        k2mm(c, ot - 1)
                k2mm(c, DT - 1)
                if c == 0:
                    te.wait_ge(gWV, 16 * DT)
                for u in range(8):
                    if 8 * c + u - 1 > 0:
                        te.wait_ge(aV, 8 * c + u - 1)
                    i, half = u // 2, u % 2
                    for dt in range(DT):
                        ins = te.matmul(
                            fb(2 + u % 2),
                            xnT[:, p * 4096 + dt * 512 + i * 128:
                                p * 4096 + dt * 512 + (i + 1) * 128],
                            wv_b[:, dt * 1024 + half * 512: dt * 1024 + (half + 1) * 512],
                            start=(dt == 0), stop=(dt == DT - 1))
                    ins.then_inc(tV, 1)
                if c < OWN_CH:
                    if c == 0:
                        te.wait_ge(gWQ, 16 * DT)
                    te.wait_ge(aK, 8 * (c + 1))
                    for ot in range(DT):
                        if 8 * c + ot - 1 > 0:
                            te.wait_ge(aQ, 8 * c + ot - 1)
                        for dt in range(DT):
                            ins = te.matmul(
                                fb(ot % 2),
                                wq_b[:, dt * 1024 + ot * 128: dt * 1024 + (ot + 1) * 128],
                                xnT[:, p * 4096 + dt * 512: p * 4096 + (dt + 1) * 512],
                                start=(dt == 0), stop=(dt == DT - 1))
                        ins.then_inc(tQ, 1)

            te.wait_ge(gI, 2)
            for c in range(NCH):
                for i in range(4):
                    t = 4 * c + i
                    te.wait_ge(vAp, t + 1)
                    if t - 1 > 0:
                        te.wait_ge(aXT, t - 1)
                    for dt in range(DT):
                        ins = te.transpose(
                            fbank6(t % 2)[:, dt * 128:(dt + 1) * 128],
                            xnb[:, (t % 4) * 1024 + dt * 128:(t % 4) * 1024 + (dt + 1) * 128],
                            ident[:, :])
                    ins.then_inc(tTR, 1)
                if c >= 1:
                    proj(c - 1)
            proj(NCH - 1)
            te.nop().then_inc(bar, 1)

    aonly.close()
    ps2 = contextlib.ExitStack()
    psF2 = ps2.enter_context(nc.psum_tensor("psF2", [128, 1024], F32))
    bscope = contextlib.ExitStack()
    sbB = lambda name, shape, dt: bscope.enter_context(nc.sbuf_tensor(name, shape, dt))

    vch = sbB("vch", [128, 5 * 1024], BF16)
    bob = sbB("bob", [128, D], BF16)
    bor = sbB("bor", [1, D], F32)
    bopr = sbB("bopr", [1, D], F32)

    mk2 = sbB("mk2", [1, 8], F32)
    cb = sbB("cb", [128, 1], F32)
    srow = bopr[0:1, 0:512]
    pT = sbB("pT", [128, KT * 512], BF16)

    aT = sbB("aT", [128, 8 * 512], BF16)
    q2r = aT[0:1, 0:T]
    qsqb = aT[:, 2048:3072]
    xo = sbB("xo", [128, 2 * 1024], F32)
    x1t = sbB("x1t", [128, 2 * 1024], F32)
    yt = sbB("yt", [128, 1024], F32)
    sR = sbB("sR", [128, 4], F32)
    rR = sbB("rR", [128, 4], F32)
    bnb2 = sbB("bnb2", [128, 12], F32)
    mv2 = sbB("mv2", [128, 2 * 2], F32)
    ln2s = sbB("ln2s", [128, 2 * 4], F32)

    Q2P, Q2L, Q2G = C.Q2P, C.Q2L, C.Q2G
    NQ2 = Q2P * Q2G

    def OBANK(u2):
        return fb((4, 5, 3)[u2 % 3])

    with nc.Block("phaseB") as blk:

        @blk.gpsimd
        def _(g):
            g.wait_ge(bar, BAR1)
            g.dma_start(out=bor[:, :], in_=bo).then_inc(gBI, 16)
            for qb in range(NQB):
                for kt in range(KT):
                    gv = qb * KT + kt
                    if gv % 2 != 1:
                        continue
                    if gv - 3 > 0:
                        g.wait_ge(tPV, gv - 3)
                    sl = 3 + (gv // 2) % 2
                    g.dma_start(out=vch[:, sl * 1024:(sl + 1) * 1024],
                                in_=vscr[kt]).then_inc(gVRs[sl - 3], 16)
            g.nop().then_inc(bar, 1)

        @blk.scalar
        def _(s):
            s.wait_ge(bar, BAR1)
            for pc in range(Q2P):
                for dt in range(DT):
                    k = pc * DT + dt
                    if k > 1:
                        s.wait_ge(tNB, k - 1)
                    s.activation(qsqb[:, (k % 2) * Q2L:(k % 2 + 1) * Q2L],
                                 qT[:, dt * T + pc * Q2L: dt * T + (pc + 1) * Q2L],
                                 AF.Square).then_inc(aNB, 1)
            s.wait_ge(vNB, NQ2 + 1)
            s.activation(mk2[0:1, 2:3], mk2[0:1, 0:1], AF.Ln)
            s.activation(mk2[0:1, 3:4], mk2[0:1, 1:2], AF.Ln).then_inc(aNB2, 1)
            s.wait_ge(vNB, NQ2 + 2)
            s.activation(mk2[0:1, 5:6], mk2[0:1, 4:5], AF.Exp, scale=0.5
                         ).then_inc(aNB2, 1)

            s.wait_ge(vNB, NQ2 + 4)
            for qb in range(NQB):
                for kt in range(KT):
                    gS = qb * KT + kt
                    s.wait_ge(tS, gS + 1)
                    if kt == 0 and qb >= 1:
                        s.wait_ge(tPV, qb * KT)
                    s.activation(pT[:, kt * 512:(kt + 1) * 512],
                                 fb(gS % 4), AF.Exp, bias=cb[:, 0:1]
                                 ).then_inc(aB1, 1)
                s.wait_ge(tPV, (qb + 1) * KT)
                if qb >= 1:
                    s.wait_ge(tO, qb * 8)
                for ct in range(8):
                    sb_ = fb(ct) if ct < 6 else psF2[:, (ct - 6) * 512:(ct - 5) * 512]
                    s.activation(aT[:, ct * 512:(ct + 1) * 512],
                                 sb_, AF.Identity).then_inc(aB2, 1)
                for qs in range(4):
                    gq = qb * 4 + qs
                    pq = qs % 2
                    s.wait_ge(vLb, gq + 1)
                    s.activation(ln2s[:, pq * 4: pq * 4 + 1],
                                 mv2[:, pq * 2 + 1: pq * 2 + 2],
                                 AF.Ln, bias=epsc[:, 0:1]).then_inc(aLa, 1)
                    s.wait_ge(aLa, gq + 1)
                    s.activation(ln2s[:, pq * 4 + 1: pq * 4 + 2],
                                 ln2s[:, pq * 4: pq * 4 + 1],
                                 AF.Exp, scale=-0.5).then_inc(aLc, 1)
                    s.wait_ge(vLc, gq + 1)
                    if gq > 0:
                        s.wait_ge(tY, gq)
                    s.activation(yt[:, 0:1024],
                                 x1t[:, pq * 1024:(pq + 1) * 1024], AF.Identity,
                                 scale=ln2s[:, pq * 4 + 1: pq * 4 + 2],
                                 bias=ln2s[:, pq * 4 + 2: pq * 4 + 3]).then_inc(aB4, 1)
            s.nop().then_inc(bar, 1)

        @blk.vector
        def _(v):
            def ystev(gq, bank):
                v.wait_ge(tY, gq + 1)
                v.tensor_copy(
                    yTb.rearrange("q (dt n) -> q dt n", n=T)[:, :, gq * 128:(gq + 1) * 128],
                    psF[:, bank * 512:(bank + 2) * 512]
                    .rearrange("q (dt n) -> q dt n", dt=DT)
                ).then_inc(vY, 1)

            v.wait_ge(bar, BAR1)
            v.wait_ge(tBO, 1)
            v.wait_ge(gBI, 64)
            v.tensor_add(bopr[0:1, :], fb(4, cols=1024, pn=1), bor[0:1, :]
                         ).then_inc(vBO, 1)
            for pc in range(Q2P):
                v.wait_ge(tNB, (pc + 1) * DT)
                for j in range(Q2G):
                    v.tensor_copy(q2r[:1, pc * Q2L + j * 512: pc * Q2L + (j + 1) * 512],
                                  fb(4 + j, pn=1)).then_inc(vNB, 1)
            v.wait_ge(vNB, NQ2)
            v.wait_ge(vK2, NCH)
            v.reduce_max(mk2[0:1, 0:1], k2m[0:1, 0:NCH], axis=AX.X)
            v.reduce_max(mk2[0:1, 1:2], q2r[0:1, :], axis=AX.X).then_inc(vNB, 1)
            v.wait_ge(aNB2, 1)
            v.tensor_add(mk2[0:1, 4:5], mk2[0:1, 2:3], mk2[0:1, 3:4]
                         ).then_inc(vNB, 1)
            v.wait_ge(aNB2, 2)
            v.tensor_scalar(mk2[0:1, 6:7], mk2[0:1, 5:6], -1.0, None, ALU.mult
                            ).then_inc(vNB, 1)
            v.wait_ge(tBO, 3)
            v.tensor_copy(cb[:, :], fb(4, cols=1)).then_inc(vNB, 1)
            v.tensor_copy(bob[:, :], psF[:, 0:1024]).then_inc(vBO, 1)

            for qb in range(NQB):
                if qb >= 1:
                    ystev(qb * 4 - 1, bank=4)
                v.wait_ge(tSs, qb + 1)
                v.tensor_copy(srow[0:1, :], fb(4, pn=1)).then_inc(vS, 1)
                v.wait_ge(tSr, qb + 1)
                v.tensor_copy(sR[:, :], fb(5, cols=4)).then_inc(vS, 1)
                v.wait_ge(vS, 3 * qb + 2)
                v.reciprocal(rR[:, :], sR[:, :]).then_inc(vS, 1)
                v.wait_ge(vS, 3 * qb + 3)
                for qs in range(4):
                    gq = qb * 4 + qs
                    pq = qs % 2
                    v.wait_ge(sXOs[gq % 2], 16 * (gq // 2 + 1))
                    v.wait_ge(tO, qb * 8 + qs * 2 + 2)
                    if gq - 1 > 0:
                        v.wait_ge(aB4, gq - 1)
                    if gq >= 2:
                        v.wait_ge(sX1Ws[gq % 2], 16 * (gq // 2))
                    osrc = psF[:, 2048:3072] if qs % 2 == 0 else psF2[:, 0:1024]
                    v.scalar_tensor_tensor(x1t[:, pq * 1024:(pq + 1) * 1024],
                                           osrc, rR[:, qs:qs + 1],
                                           xo[:, pq * 1024:(pq + 1) * 1024],
                                           ALU.mult, ALU.add).then_inc(vXB, 1)
                    v.wait_ge(vXB, gq + 1)
                    v.tensor_add(x1t[:, pq * 1024:(pq + 1) * 1024],
                                 x1t[:, pq * 1024:(pq + 1) * 1024], bob[:, :]
                                 ).then_inc(vX1, 1)
                    v.wait_ge(vX1, gq + 1)
                    if gq > 0:
                        v.wait_ge(vLb, gq)
                    v.bn_stats(bnb2[:, 0:6],
                               x1t[:, pq * 1024: pq * 1024 + 512])
                    v.bn_stats(bnb2[:, 6:12],
                               x1t[:, pq * 1024 + 512:(pq + 1) * 1024]).then_inc(vLa, 1)
                    v.wait_ge(vLa, gq + 1)
                    v.bn_aggr(mv2[:, pq * 2:(pq + 1) * 2],
                              bnb2[:, 0:12]).then_inc(vLb, 1)
                    v.wait_ge(aLc, gq + 1)
                    v.scalar_tensor_tensor(ln2s[:, pq * 4 + 2: pq * 4 + 3],
                                           mv2[:, pq * 2: pq * 2 + 1], -1.0,
                                           ln2s[:, pq * 4 + 1: pq * 4 + 2],
                                           ALU.mult, ALU.mult).then_inc(vLc, 1)
                    if qs >= 1:
                        ystev(gq - 1, bank=(0 if qs <= 2 else 4))
            ystev(NQB * 4 - 1, bank=4)
            v.nop().then_inc(bar, 1)

        @blk.sync
        def _(sy):
            sy.wait_ge(bar, BAR1)

            def kchload(qb, c, skip_first=False):
                for hh in range((1 if skip_first else 0), 2):
                    gh = (qb * NCH + c) * 2 + hh
                    if 2 * gh - 4 > 0:
                        sy.wait_ge(tS, 2 * gh - 4)
                    sy.dma_start(out=kch[:, (gh % 3) * 2048:(gh % 3 + 1) * 2048]
                                 .rearrange("q (dt n) -> q dt n", dt=DT),
                                 in_=kscr[c][:, :, hh * 256:(hh + 1) * 256]
                                 .rearrange("dt p n -> p dt n")
                                 ).then_inc(sKRs[gh % 3], 16)

            for qb in range(NQB):
                if qb == 0:
                    kchload(0, 1, skip_first=True)
                    for c in range(2, NCH):
                        kchload(qb, c)
                else:
                    for c in range(1, NCH):
                        kchload(qb, c)
                for kt in range(KT):
                    gv = qb * KT + kt
                    if gv % 2 != 0:
                        continue
                    if gv - 4 > 0:
                        sy.wait_ge(tPV, gv - 4)
                    sl = (gv // 2) % 3
                    sy.dma_start(out=vch[:, sl * 1024:(sl + 1) * 1024],
                                 in_=vscr[kt]).then_inc(sVRs[sl], 16)
                for qs in range(4):
                    gq = qb * 4 + qs
                    if gq - 1 > 0:
                        sy.wait_ge(vXB, gq - 1)
                    sy.dma_start(out=xo[:, (qs % 2) * 1024:(qs % 2 + 1) * 1024],
                                 in_=xin_own[gq]).then_inc(sXOs[gq % 2], 16)
                if qb + 1 < NQB:
                    kchload(qb + 1, 0)
                for qs in range(4):
                    gq = qb * 4 + qs
                    sy.wait_ge(vX1, gq + 1)
                    sy.dma_start(out=x1scr[gq],
                                 in_=x1t[:, (qs % 2) * 1024:(qs % 2 + 1) * 1024]
                                 ).then_inc(sX1Ws[gq % 2], 16)
            for sl in range(2):
                sy.wait_ge(sX1Ws[sl], 16 * (4 * NQB // 2))
            sy.nop().then_inc(bar, 1)

        @blk.tensor
        def _(te):
            def ytr(gq, bank):
                pq = gq % 2
                te.wait_ge(aB4, gq + 1)
                if gq > 0:
                    te.wait_ge(vY, gq)
                for dt in range(DT):
                    ins = te.transpose(
                        psF[:, bank * 512 + dt * 128: bank * 512 + (dt + 1) * 128],
                        yt[:, dt * 128:(dt + 1) * 128],
                        identf[:, :])
                ins.then_inc(tY, 1)

            te.wait_ge(bar, BAR1)
            te.wait_ge(gWO, 16 * DT)
            te.wait_ge(vK2, NCH)
            for dt in range(DT):
                for h2 in range(2):
                    ins = te.matmul(fb(4 + h2, pn=1), bvT[:, dt:dt + 1],
                                    wo_b[:, dt * 1024 + h2 * 512: dt * 1024 + (h2 + 1) * 512],
                                    start=(dt == 0), stop=(dt == DT - 1))
            ins.then_inc(tBO, 1)
            te.wait_ge(vBO, 1)
            for pc in range(Q2P):
                if pc >= 1:
                    te.wait_ge(vNB, pc * Q2G)
                for dt in range(DT):
                    k = pc * DT + dt
                    te.wait_ge(aNB, k + 1)
                    ins = te.matmul(fb(4, pn=1), onescol[:, :],
                                    qsqb[:, (k % 2) * Q2L:(k % 2 + 1) * Q2L],
                                    start=(dt == 0), stop=(dt == DT - 1))
                    ins.then_inc(tNB, 1)
            for j in range(2):
                ins = te.matmul(psF[:, j * 512:(j + 1) * 512], ones1f[:, :],
                                bopr[0:1, j * 512:(j + 1) * 512],
                                start=True, stop=True)
            ins.then_inc(tBO, 1)
            te.wait_ge(vNB, NQ2 + 3)
            te.matmul(fb(4, cols=1), ones1f[:, :], mk2[0:1, 6:7],
                      start=True, stop=True).then_inc(tBO, 1)
            te.wait_ge(vNB, NQ2 + 4)
            te.wait_ge(vBO, 2)

            SROW0 = min(9, KT - 1)
            for qb in range(NQB):
                if qb >= 1:
                    te.wait_ge(aB2, qb * 8)
                if qb >= 1:
                    te.wait_ge(vY, qb * 4 - 2)
                for kt in range(KT):
                    if qb >= 1 and kt in (2, 3):
                        if kt == 2:
                            te.wait_ge(vX1, qb * 4)
                        ytr(qb * 4 - 2 + (kt - 2), bank=4)
                    c = kt // 4
                    gS = qb * KT + kt
                    gh = (qb * NCH + c) * 2 + (kt % 4) // 2
                    if kt % 2 == 0:
                        te.wait_ge(sKRs[gh % 3], 16 * (gh // 3 + 1))
                    if gS - 3 > 0:
                        te.wait_ge(aB1, gS - 3)
                    for dt in range(DT):
                        ins = te.matmul(fb(gS % 4),
                                        kch[:, (gh % 3) * 2048 + dt * 256 + (kt % 2) * 128:
                                            (gh % 3) * 2048 + dt * 256 + (kt % 2 + 1) * 128],
                                        qT[:, dt * T + qb * 512: dt * T + (qb + 1) * 512],
                                        start=(dt == 0), stop=(dt == DT - 1))
                    ins.then_inc(tS, 1)
                    if kt >= SROW0:
                        if kt == SROW0 and qb >= 1:
                            te.wait_ge(vX1, qb * 4)
                            te.wait_ge(vY, qb * 4)
                        te.wait_ge(aB1, qb * KT + kt)
                        te.matmul(fb(4, pn=1), onescol[:, :],
                                  pT[:, (kt - 1) * 512: kt * 512],
                                  start=(kt == SROW0), stop=False)
                te.wait_ge(aB1, (qb + 1) * KT)
                te.matmul(fb(4, pn=1), onescol[:, :],
                          pT[:, (KT - 1) * 512: KT * 512],
                          start=False, stop=False)
                for j in range(SROW0 - 1):
                    ins = te.matmul(fb(4, pn=1), onescol[:, :],
                                    pT[:, j * 512:(j + 1) * 512],
                                    start=False, stop=(j == SROW0 - 2))
                ins.then_inc(tSs, 1)
                te.wait_ge(vS, 3 * qb + 1)
                for qs in range(4):
                    ins = te.matmul(fb(5, cols=1, off=qs),
                                    srow[0:1, qs * 128:(qs + 1) * 128],
                                    ones1f[0:1, 0:1], start=True, stop=True)
                ins.then_inc(tSr, 1)
                te.wait_ge(vS, 3 * qb + 2)
                for kt in range(KT):
                    gv = qb * KT + kt
                    if gv % 2 == 0:
                        sl = (gv // 2) % 3
                        te.wait_ge(sVRs[sl], 16 * (gv // 6 + 1))
                    else:
                        sl = 3 + (gv // 2) % 2
                        te.wait_ge(gVRs[sl - 3], 16 * (gv // 4 + 1))
                    for ct in range(8):
                        ob = fb(ct) if ct < 6 else psF2[:, (ct - 6) * 512:(ct - 5) * 512]
                        ins = te.matmul(
                            ob,
                            vch[:, sl * 1024 + ct * 128: sl * 1024 + (ct + 1) * 128],
                            pT[:, kt * 512:(kt + 1) * 512],
                            start=(kt == 0), stop=(kt == KT - 1))
                    ins.then_inc(tPV, 1)
                te.wait_ge(aB2, qb * 8 + 8)
                te.wait_ge(vS, 3 * qb + 2)
                for qs in range(4):
                    if qs <= 1:
                        if qb >= 1:
                            te.wait_ge(vX1, qb * 4)
                    else:
                        te.wait_ge(vX1, qb * 4 + qs - 1)
                    for ch in range(2):
                        ob = fb(4 + ch) if qs % 2 == 0 else psF2[:, ch * 512:(ch + 1) * 512]
                        for ct in range(DT):
                            ins = te.matmul(
                                ob,
                                aT[:, ct * 512 + qs * 128: ct * 512 + (qs + 1) * 128],
                                wo_b[:, ct * 1024 + ch * 512: ct * 1024 + (ch + 1) * 512],
                                start=(ct == 0), stop=(ct == DT - 1))
                        ins.then_inc(tO, 1)
                    if 1 <= qs <= 2:
                        ytr(qb * 4 + qs - 1, bank=0)
                if qb == NQB - 1:
                    te.wait_ge(vX1, NQB * 4)
                    ytr(qb * 4 + 2, bank=4)
                    ytr(qb * 4 + 3, bank=4)
            te.nop().then_inc(bar, 1)

    bscope.close()
    ab.close()

    cscope = contextlib.ExitStack()
    sbC = lambda name, shape, dt: cscope.enter_context(nc.sbuf_tensor(name, shape, dt))
    acc = sbC("acc", [128, NQT * 1024], F32)
    zb = sbC("zb", [128, HC * T], BF16)
    HH = H // NSPE
    w1b = sbC("w1b", [128, 2 * DT * HH], BF16)
    w2b = sbC("w2b", [128, 2 * DT * HH], BF16)
    w3b = sbC("w3b", [128, 2 * HC * 1024], BF16)
    silu_st = sbC("silu_st", [128, 2 * 512], BF16)
    wgb = sbC("wgb", [128, DT * E], BF16)
    comb = sbC("comb", [128, NQT * E], F32)
    gm1 = sbC("gm1", [128, NQT], F32)
    gm2 = sbC("gm2", [128, NQT], F32)
    gd = sbC("gd", [128, NQT], F32)
    gw1 = sbC("gw1", [128, NQT], F32)
    gw2 = sbC("gw2", [128, NQT], F32)
    geq = sbC("geq", [128, NQT * E], F32)
    geq2 = sbC("geq2", [128, NQT * E], F32)
    gmsk = sbC("gmsk", [128, NQT * E], F32)


    with nc.Block("phaseC") as blk:

        @blk.gpsimd
        def _(g):
            g.wait_ge(bar, BAR2)
            g.dma_start(out=wgb[:, :], in_=Wg).then_inc(gWG, 16)
            for dt in range(DT):
                g.dma_start(out=w1b[:, dt * HH:(dt + 1) * HH],
                            in_=w1v[0, dt][:, 0:HH]).then_inc(gW1s[0], 16)
            for dt in range(DT):
                g.dma_start(out=w2b[:, dt * HH:(dt + 1) * HH],
                            in_=w2v[0, dt][:, 0:HH]).then_inc(gW2s[0], 16)
            for e in range(E):
                for sh in range(NSPE):
                    sub = NSPE * e + sh
                    if sub >= 1:
                        if sub >= 2:
                            g.wait_ge(tH, (sub - 1) * UH1)
                        for dt in range(DT):
                            g.dma_start(out=w1b[:, (sub % 2) * DT * HH + dt * HH:
                                               (sub % 2) * DT * HH + (dt + 1) * HH],
                                        in_=w1v[e, dt][:, sh * HH:(sh + 1) * HH]
                                        ).then_inc(gW1s[sub % 2], 16)
                        for dt in range(DT):
                            g.dma_start(out=w2b[:, (sub % 2) * DT * HH + dt * HH:
                                               (sub % 2) * DT * HH + (dt + 1) * HH],
                                        in_=w2v[e, dt][:, sh * HH:(sh + 1) * HH]
                                        ).then_inc(gW2s[sub % 2], 16)
                    if sub >= 2:
                        g.wait_ge(tO2, (sub - 1) * UO2)
                    for ht in range(HC):
                        g.dma_start(out=w3b[:, (sub % 2) * HC * 1024 + ht * 1024:
                                           (sub % 2) * HC * 1024 + (ht + 1) * 1024],
                                    in_=w3v[e, sh * HC + ht]).then_inc(gW3s[sub % 2], 16)
            g.wait_ge(gWG, 16)
            for sl in range(2):
                g.wait_ge(gW1s[sl], 16 * DT * (NSUB // 2))
                g.wait_ge(gW2s[sl], 16 * DT * (NSUB // 2))
                g.wait_ge(gW3s[sl], 16 * HC * (NSUB // 2))
            g.nop().then_inc(bar, 1)

        @blk.sync
        def _(sy):
            sy.wait_ge(bar, BAR2)
            npiece = max(1, NQT // 4)
            for j in range(NQT // npiece):
                sy.dma_start(out=acc[:, j * npiece * 1024:(j + 1) * npiece * 1024]
                             .rearrange("p (t n) -> p t n", n=1024),
                             in_=x1scr[j * npiece:(j + 1) * npiece]
                             .rearrange("t p n -> p t n")).then_inc(sACC, 16)
            for tt in range(NQT):
                sy.wait_ge(vAcc, (NSUB - 1) * UO2 + tt * 2 + 2)
                sy.dma_start(out=outv[tt], in_=acc[:, tt * 1024:(tt + 1) * 1024]
                             ).then_inc(sOUT, 16)
            sy.wait_ge(sOUT, 16 * NQT)

        @blk.tensor
        def _(te):
            te.wait_ge(bar, BAR2)
            te.wait_ge(gWG, 16)
            for tt in range(NQT):
                for dt in range(DT):
                    ins = te.matmul(fb(4, cols=E, off=E * tt),
                                    yTb[:, dt * T + tt * 128: dt * T + (tt + 1) * 128],
                                    wgb[:, dt * E:(dt + 1) * E],
                                    start=(dt == 0), stop=(dt == DT - 1))
                ins.then_inc(tG, 1)
            for e in range(E):
                for sh in range(NSPE):
                    sub = NSPE * e + sh
                    te.wait_ge(gW1s[sub % 2], 16 * DT * (sub // 2 + 1))
                    te.wait_ge(gW2s[sub % 2], 16 * DT * (sub // 2 + 1))
                    if sub >= 1:
                        last3 = UO2 - 1 - ((UO2 - 3) % 3)
                        te.wait_ge(vAcc, (sub - 1) * UO2 + last3 + 1)
                    for u in range(UH1):
                        tq, ht = divmod(u, HC)
                        gu = sub * UH1 + u
                        if gu - 2 >= 0:
                            te.wait_ge(aSl, gu - 1)
                            te.wait_ge(vZ, gu - 1)
                        for dt in range(DT):
                            te.matmul(
                                fb(2 * (u % 2)),
                                w1b[:, (sub % 2) * DT * HH + dt * HH + ht * 128:
                                    (sub % 2) * DT * HH + dt * HH + (ht + 1) * 128],
                                yTb[:, dt * T + tq * 512: dt * T + (tq + 1) * 512],
                                start=(dt == 0), stop=(dt == DT - 1))
                        for dt in range(DT):
                            ins = te.matmul(
                                fb(2 * (u % 2) + 1),
                                w2b[:, (sub % 2) * DT * HH + dt * HH + ht * 128:
                                    (sub % 2) * DT * HH + dt * HH + (ht + 1) * 128],
                                yTb[:, dt * T + tq * 512: dt * T + (tq + 1) * 512],
                                start=(dt == 0), stop=(dt == DT - 1))
                        ins.then_inc(tH, 1)
                    te.wait_ge(gW3s[sub % 2], 16 * HC * (sub // 2 + 1))
                    if sub == 0:
                        te.wait_ge(vG, 9)
                    for u2 in range(UO2):
                        tt, ch = divmod(u2, 2)
                        gu2 = sub * UO2 + u2
                        if ch == 0 and tt % 4 == 0:
                            te.wait_ge(vZ, sub * UH1 + (tt // 4 + 1) * HC)
                        if u2 == 2:
                            te.wait_ge(vZ, (sub + 1) * UH1)
                        if gu2 - 3 >= 0:
                            te.wait_ge(vAcc, gu2 - 2)
                        for ht in range(HC):
                            ins = te.matmul(
                                OBANK(u2),
                                zb[:, ht * T + tt * 128: ht * T + (tt + 1) * 128],
                                w3b[:, (sub % 2) * HC * 1024 + ht * 1024 + ch * 512:
                                    (sub % 2) * HC * 1024 + ht * 1024 + (ch + 1) * 512],
                                start=(ht == 0), stop=(ht == HC - 1))
                        ins.then_inc(tO2, 1)
            te.nop().then_inc(bar, 1)

        @blk.scalar
        def _(s):
            s.wait_ge(bar, BAR2)
            s.wait_ge(vG, 5)
            s.activation(gd[:, :], gd[:, :], AF.Exp, scale=-1.0).then_inc(aG, 1)
            for sub in range(NSUB):
                for u in range(UH1):
                    gu = sub * UH1 + u
                    s.wait_ge(tH, gu + 1)
                    if gu - 2 >= 0:
                        s.wait_ge(vZ, gu - 1)
                    s.activation(silu_st[:, (u % 2) * 512:(u % 2 + 1) * 512],
                                 fb(2 * (u % 2)), AF.Silu).then_inc(aSl, 1)
            s.nop().then_inc(bar, 1)

        @blk.vector
        def _(v):
            v.wait_ge(bar, BAR2)
            v.wait_ge(tG, NQT)
            for tt in range(NQT):
                ins = v.reduce_max(gm1[:, tt:tt + 1], fb(4, cols=E, off=E * tt),
                                   axis=AX.X)
            ins.then_inc(vG, 1)
            v.wait_ge(vG, 1)
            for tt in range(NQT):
                ins = v.tensor_single_scalar(geq[:, E * tt:E * (tt + 1)],
                                             fb(4, cols=E, off=E * tt),
                                             gm1[:, tt:tt + 1], ALU.is_equal)
            ins.then_inc(vGb, 1)
            v.wait_ge(vGb, 1)
            for tt in range(NQT):
                ins = v.scalar_tensor_tensor(gmsk[:, E * tt:E * (tt + 1)],
                                             geq[:, E * tt:E * (tt + 1)], -1e30,
                                             fb(4, cols=E, off=E * tt),
                                             ALU.mult, ALU.add)
            ins.then_inc(vG, 1)
            v.wait_ge(vG, 2)
            for tt in range(NQT):
                ins = v.reduce_max(gm2[:, tt:tt + 1], gmsk[:, E * tt:E * (tt + 1)],
                                   axis=AX.X)
            ins.then_inc(vG, 1)
            v.wait_ge(vG, 3)
            for tt in range(NQT):
                ins = v.tensor_single_scalar(geq2[:, E * tt:E * (tt + 1)],
                                             gmsk[:, E * tt:E * (tt + 1)],
                                             gm2[:, tt:tt + 1], ALU.is_equal)
            ins.then_inc(vG, 1)
            v.wait_ge(vG, 4)
            v.tensor_sub(gd[:, :], gm1[:, :], gm2[:, :]).then_inc(vG, 1)
            v.wait_ge(aG, 1)
            v.tensor_scalar(gw1[:, :], gd[:, :], 1.0, None, ALU.add).then_inc(vG, 1)
            v.wait_ge(vG, 6)
            v.reciprocal(gw1[:, :], gw1[:, :]).then_inc(vG, 1)
            v.wait_ge(vG, 7)
            v.tensor_scalar(gw2[:, :], gw1[:, :], -1.0, 1.0, ALU.mult, ALU.add
                            ).then_inc(vG, 1)
            v.wait_ge(vG, 8)
            for tt in range(NQT):
                ins = v.tensor_scalar(comb[:, E * tt:E * (tt + 1)],
                                      geq[:, E * tt:E * (tt + 1)],
                                      gw1[:, tt:tt + 1], None, ALU.mult)
            ins.then_inc(vGb, 1)
            v.wait_ge(vGb, 2)
            for tt in range(NQT):
                ins = v.scalar_tensor_tensor(comb[:, E * tt:E * (tt + 1)],
                                             geq2[:, E * tt:E * (tt + 1)],
                                             gw2[:, tt:tt + 1],
                                             comb[:, E * tt:E * (tt + 1)],
                                             ALU.mult, ALU.add)
            ins.then_inc(vG, 1)
            npiece = max(1, NQT // 4)
            v.wait_ge(sACC, 16 * (NQT // npiece))
            for sub in range(NSUB):
                e = sub // NSPE
                for u in range(UH1):
                    tq, ht = divmod(u, HC)
                    gu = sub * UH1 + u
                    v.wait_ge(aSl, gu + 1)
                    if sub >= 1 and u == 0:
                        v.wait_ge(tO2, sub * UO2)
                    v.tensor_tensor(zb[:, ht * T + tq * 512: ht * T + (tq + 1) * 512],
                                    silu_st[:, (u % 2) * 512:(u % 2 + 1) * 512],
                                    fb(2 * (u % 2) + 1), ALU.mult).then_inc(vZ, 1)
                for u2 in range(UO2):
                    tt, ch = divmod(u2, 2)
                    gu2 = sub * UO2 + u2
                    v.wait_ge(tO2, gu2 + 1)
                    v.scalar_tensor_tensor(
                        acc[:, tt * 1024 + ch * 512: tt * 1024 + (ch + 1) * 512],
                        OBANK(u2),
                        comb[:, tt * E + e: tt * E + e + 1],
                        acc[:, tt * 1024 + ch * 512: tt * 1024 + (ch + 1) * 512],
                        ALU.mult, ALU.add).then_inc(vAcc, 1)
            v.nop().then_inc(bar, 1)

    cscope.close()
    ps2.close()
    stack.close()
    return nc


_CACHE = {}


def _get_nc():
    if "full" not in _CACHE:
        _CACHE["full"] = build(Cfg())
    return _CACHE["full"]


def _shard(x, Wq, bq_, Wk, bk_, Wv, bv_, Wo, bo_, Wg_, W1, W2, W3, e):
    B, TT, _ = x.shape
    TO = TT // 4

    def bias_layout(b):
        return np.ascontiguousarray(np.asarray(b, np.float32).reshape(DT, 128).T)

    wg = np.asarray(Wg_, np.float32)
    wg_dev = np.ascontiguousarray(
        wg.reshape(DT, 128, e).transpose(1, 0, 2).reshape(128, DT * e))
    common = {
        "Wq": np.ascontiguousarray(np.asarray(Wq, np.float32)),
        "Wk": np.ascontiguousarray(np.asarray(Wk, np.float32)),
        "Wv": np.ascontiguousarray(np.asarray(Wv, np.float32)),
        "Wo": np.ascontiguousarray(np.asarray(Wo, np.float32)),
        "bq": bias_layout(bq_), "bk": bias_layout(bk_), "bv": bias_layout(bv_),
        "bo": np.ascontiguousarray(np.asarray(bo_, np.float32).reshape(1, D)),
        "Wg": wg_dev,
        "W1": np.ascontiguousarray(np.asarray(W1, np.float32)),
        "W2": np.ascontiguousarray(np.asarray(W2, np.float32)),
        "W3": np.ascontiguousarray(np.asarray(W3, np.float32)),
    }
    in_maps = []
    for core in range(NCORES):
        b, q = core // 4, core % 4
        m = dict(common)
        m["x_own"] = np.ascontiguousarray(x[b, q * TO:(q + 1) * TO])
        m["x_rest"] = np.ascontiguousarray(
            np.concatenate([x[b, :q * TO], x[b, (q + 1) * TO:]], axis=0))
        in_maps.append(m)
    return in_maps


def kernel(L0_attn_x, L0_attn_Wq, L0_attn_bq, L0_attn_Wk, L0_attn_bk,
           L0_attn_Wv, L0_attn_bv, L0_attn_Wo, L0_attn_bo,
           L0_ffn_Wg, L0_ffn_W1, L0_ffn_W2, L0_ffn_W3):
    x = np.ascontiguousarray(np.asarray(L0_attn_x, dtype=np.float32))
    B, TT, _ = x.shape
    TO = TT // 4
    in_maps = _shard(x, L0_attn_Wq, L0_attn_bq, L0_attn_Wk, L0_attn_bk,
                     L0_attn_Wv, L0_attn_bv, L0_attn_Wo, L0_attn_bo,
                     L0_ffn_Wg, L0_ffn_W1, L0_ffn_W2, L0_ffn_W3, 8)
    nc = _get_nc()
    res = run_bass_kernel_spmd(nc, in_maps, core_ids=list(range(NCORES)))
    outp = np.empty((B, TT, D), np.float32)
    for core in range(NCORES):
        b, q = core // 4, core % 4
        outp[b, q * TO:(q + 1) * TO] = res.results[core]["out"]
    return outp
